# version 18
# baseline (speedup 1.0000x reference)
import numpy as np
import ml_dtypes
from contextlib import ExitStack

import concourse.bass as bass
import concourse.mybir as mybir
from concourse.bass_utils import run_bass_kernel_spmd

F32 = mybir.dt.float32
BF16 = mybir.dt.bfloat16
AF = mybir.ActivationFunctionType
ALU = mybir.AluOpType
AX = mybir.AxisListType
BF = ml_dtypes.bfloat16

HID = 1024
NP = 4
C = 64
QK_ROWS = {(0, 0): 0, (1, 0): 32, (0, 1): 64, (1, 1): 96}


def _swap32():
    sw = np.zeros((128, 128), np.float32)
    for m in range(128):
        base = (m // 64) * 64
        sw[m, base + (m % 64 + 32) % 64] = 1
    return sw


def _perm_idx(hbase):
    ev, od = np.arange(0, 64, 2), np.arange(1, 64, 2)
    qk, v = [], []
    for hp in range(NP):
        h0, h1 = hbase + 2 * hp, hbase + 2 * hp + 1
        qk.append(np.concatenate([64 * h0 + ev, 64 * h0 + od,
                                  64 * h1 + ev, 64 * h1 + od]))
        v.append(np.concatenate([64 * h0 + np.arange(64), 64 * h1 + np.arange(64)]))
    return qk, v


def pack_attn(inputs, hh):
    g = lambda n: np.asarray(inputs[n], dtype=np.float32)
    hbase = hh * 8
    qk_idx, v_idx = _perm_idx(hbase)
    rows = []
    for hp in range(NP):
        rows.append(g("L0_attn_Wq")[qk_idx[hp]])
    for hp in range(NP):
        rows.append(g("L0_attn_Wk")[qk_idx[hp]])
    for hp in range(NP):
        rows.append(g("L0_attn_Wv")[v_idx[hp]])
    for hp in range(NP):
        rows.append(g("L0_attn_Wg")[v_idx[hp]])
    rows.append(g("L0_attn_Wgk")[hbase:hbase + 8])
    rows.append(g("L0_attn_Wb")[hbase:hbase + 8])
    WA = np.concatenate(rows, 0)
    wa = np.ascontiguousarray(WA.T).reshape(HID // 128, 128, 2064)
    cw = np.zeros((128, 48), np.float32)
    cb = np.zeros((128, 12), np.float32)
    cws = [g("L0_attn_cqw"), g("L0_attn_ckw"), g("L0_attn_cvw")]
    cbs = [g("L0_attn_cqb"), g("L0_attn_ckb"), g("L0_attn_cvb")]
    for qi in range(3):
        for hp in range(NP):
            idx = qk_idx[hp] if qi < 2 else v_idx[hp]
            for tap in range(4):
                cw[:, qi * 16 + hp * 4 + tap] = cws[qi][idx, 0, tap]
            cb[:, qi * 4 + hp] = cbs[qi][idx]
    woT = np.ascontiguousarray(g("L0_attn_Wo").T[64 * hbase: 64 * hbase + 512, :]
                               ).reshape(4, 128, 1024)
    sl = lambda n: np.ascontiguousarray(g(n)[hbase:hbase + 8].reshape(8, 1))
    return dict(wa=wa, cw=cw, cb=cb, wo=woT,
                bgk8=sl("L0_attn_bgk"), dtb8=sl("L0_attn_dt_bias"),
                bb8=sl("L0_attn_bb"), alog8=sl("L0_attn_A_log"),
                dp8=sl("L0_attn_Dp"),
                onorm=np.ascontiguousarray(g("L0_attn_onorm_w").reshape(1, 64)))


def rope_tables(T):
    i = np.arange(32)
    inv_freq = 10000.0 ** (-i / 32.0)
    jj = (2 * np.arange(32)) % 32
    t = np.arange(T, dtype=np.float64)
    ang = t[None, :] * inv_freq[jj][:, None]
    c = np.cos(ang); s = np.sin(ang)
    cdup = np.concatenate([c, c], 0)
    CSA = np.concatenate([cdup, cdup], 0).astype(ml_dtypes.bfloat16)
    CSB = np.concatenate([-s, s, -s, s], 0).astype(ml_dtypes.bfloat16)
    return np.ascontiguousarray(CSA), np.ascontiguousarray(CSB)


def pack_ffn(inputs):
    g = lambda n: np.asarray(inputs[n], dtype=np.float32)
    ch = lambda w: np.ascontiguousarray(w).reshape(w.shape[0] // 128, 128, w.shape[1])
    return dict(
        fw1=ch(g("L0_ffn_cn_w1")), fb1=g("L0_ffn_cn_b1").reshape(1, 1024),
        fw2=ch(g("L0_ffn_cn_w2")), fb2=g("L0_ffn_cn_b2").reshape(1, 512),
        fw3=ch(g("L0_ffn_cn_w3")), fb3=g("L0_ffn_cn_b3").reshape(1, 256),
        fw4=ch(g("L0_ffn_cn_w4")), fb4=g("L0_ffn_cn_b4").reshape(1, 1),
        gw1=ch(g("L0_ffn_gc_w1")), gb1=g("L0_ffn_gc_b1").reshape(1, 512),
        gw2=ch(g("L0_ffn_gc_w2")), gb2=g("L0_ffn_gc_b2").reshape(1, 256),
        gw3=ch(g("L0_ffn_gc_w3")), gb3=g("L0_ffn_gc_b3").reshape(1, 1),
        arange=g("L0_ffn_adaptive_range").reshape(1, 1))


class Builder:
    def __init__(self, T, W, debug=()):
        assert W % 128 == 0 and T % W == 0 and W % C == 0 and 128 <= W <= 512
        self.T, self.W = T, W
        self.NW = T // W
        self.NCH = W // C
        self.NTW = W // 128
        self.TO = T // 2
        self.NTO = self.TO // 128
        self.debug = set(debug)
        self.nc = bass.Bass("TRN2", target_bir_lowering=False, num_devices=8)
        self.cnt = {}
        self.pb_free = {}
        self._pe_last = None
        self._pe_cols = {0: None, 1: None}
        self._pe_drains = 0
        self.es = ExitStack()
        self.blk = None

    def tok(self, s):
        return (s, self.cnt.get(s.name, 0))

    def _inc(self, ins, s, n=1):
        ins.then_inc(s, n)
        self.cnt[s.name] = self.cnt.get(s.name, 0) + n

    @staticmethod
    def _wait(eng, token):
        s, v = token
        if v > 0:
            eng.wait_ge(s, v)

    _DRAIN_OPS = frozenset((
        "tensor_copy", "tensor_tensor", "tensor_add", "tensor_sub",
        "tensor_mul", "tensor_max", "tensor_scalar", "tensor_single_scalar",
        "tensor_scalar_add", "tensor_scalar_sub", "tensor_scalar_mul",
        "tensor_scalar_max", "tensor_scalar_min", "tensor_relu",
        "tensor_reduce", "reduce_max", "reduce_sum", "scalar_tensor_tensor",
        "tensor_tensor_scan", "memset", "memzero", "reciprocal", "activation",
        "copy", "mul", "add", "square", "sqrt", "bn_stats", "bn_aggr",
        "select", "transpose", "iota"))

    class _Drainer:

        def __init__(self, eng):
            self._e = eng
            self._n = 0
            self._outs = set()
            self._ins = set()
            self.last = None

        @staticmethod
        def _tnames(args, kwargs):
            got = []
            for x in list(args) + list(kwargs.values()):
                t = getattr(x, "tensor", None)
                if t is not None:
                    got.append(t.name)
            return got

        def __getattr__(self, name):
            fn = getattr(self._e, name)
            if name in Builder._DRAIN_OPS:
                def wrapped(*a, **k):
                    names = self._tnames(a, k)
                    out = set(names[:1])
                    ins = set(names[1:])
                    if (out & (self._outs | self._ins)) or (ins & self._outs):
                        self._e.drain()
                        self._outs.clear()
                        self._ins.clear()
                    self._outs |= out
                    self._ins |= ins
                    self._n += 1
                    self.last = fn(*a, **k)
                    return self.last
                return wrapped
            return fn

    class _PEGuard:

        def __init__(self, eng, owner):
            self._e = eng
            self._o = owner

        def matmul(self, out, lhsT, rhs, *a, **k):
            K = lhsT.shape[0]
            M = lhsT.shape[1]
            tp = k.get("tile_position", (0, 0))
            row = "F" if K > 64 else tp[0] // 64
            cols = (0, 1) if (K > 64 or M > 64) else (tp[1] // 64,)
            st = self._o._pe_cols
            if any(st[c] is not None and st[c] != row for c in cols):
                self._e.drain()
                self._o._pe_drains += 1
                st[0] = st[1] = None
            for c in cols:
                st[c] = row
            return self._e.matmul(out, lhsT, rhs, *a, **k)

        def __getattr__(self, name):
            return getattr(self._e, name)

    def step(self, eng, waits, fn, dma=False):
        S = self.S
        sem = {"t": S["T"], "v": S["V"], "a": S["A"], "g": S["G"],
               "s": S["SY"]}[eng]
        attr = {"t": "tensor", "v": "vector", "a": "scalar", "g": "gpsimd",
                "s": "sync"}[eng]
        me = self

        def body(e):
            for tkn in waits:
                me._wait(e, tkn)
            if eng in ("v", "a", "g"):
                ep = Builder._Drainer(e)
            elif eng == "t":
                ep = Builder._PEGuard(e, me)
            else:
                ep = e
            last = fn(ep)
            if dma:
                assert last is None
            else:
                if eng in ("v", "a", "g"):
                    if ep._n == 1 and ep.last is not None:
                        last = ep.last
                    else:
                        last = e.drain()
                elif last is None:
                    last = e.nop()
                me._inc(last, sem, 1)
        getattr(self.blk, attr)(body)
        return self.tok(sem)

    def dma(self, eng, waits, fn):
        S = self.S
        sem = S["gld"] if eng == "g" else S["ld"]
        me = self
        fence = self.tok(sem)

        def body(e):
            me._wait(e, fence)
            for tkn in waits:
                me._wait(e, tkn)
            for ins in fn(e):
                me._inc(ins, sem, 16)
        getattr(self.blk, {"s": "sync", "g": "gpsimd", "a": "scalar"}[eng])(body)
        return self.tok(sem)

    def bank(self, i):
        return self.pb_free.get(i, (self.S["V"], 0))

    def build(self):
        nc = self.nc
        T, W, NCH = self.T, self.W, self.NCH
        TO = self.TO
        es = self.es

        din = lambda n, s: nc.dram_tensor(n, s, F32, kind="ExternalInput").ap()
        self.x_in = din("x_in", [T, HID])
        self.xh_in = din("xh_in", [TO, HID])
        self.wa_in = din("wa", [HID // 128, 128, 2064])
        self.wo_in = din("wo", [4, 128, 1024])
        self.cw_in = din("cw", [128, 48])
        self.cb_in = din("cb", [128, 12])
        self.csa_in = nc.dram_tensor("csa", [128, T], BF16,
                                     kind="ExternalInput").ap()
        self.csb_in = nc.dram_tensor("csb", [128, T], BF16,
                                     kind="ExternalInput").ap()
        self.small_in = {n: din(n, [8, 1])
                         for n in ("bgk8", "dtb8", "bb8", "alog8", "dp8")}
        self.onorm_in = din("onorm", [1, 64])
        self.ffn_in = {n: din(n, s) for n, s in (
            ("fw1", [8, 128, 1024]), ("fb1", [1, 1024]),
            ("fw2", [8, 128, 512]), ("fb2", [1, 512]),
            ("fw3", [4, 128, 256]), ("fb3", [1, 256]),
            ("fw4", [2, 128, 1]), ("fb4", [1, 1]),
            ("gw1", [8, 128, 512]), ("gb1", [1, 512]),
            ("gw2", [4, 128, 256]), ("gb2", [1, 256]),
            ("gw3", [2, 128, 1]), ("gb3", [1, 1]),
            ("arange", [1, 1]))}
        self.out_t = nc.dram_tensor("out", [HID, TO], F32, kind="ExternalOutput").ap()
        self.psh = nc.dram_tensor("psh", [2, T, HID], F32, addr_space="Shared")
        self.x2buf = nc.dram_tensor("x2buf", [TO, HID], F32)
        self.shsum = nc.dram_tensor("shsum", [2, 1, HID], F32, addr_space="Shared")

        eye64 = np.eye(64, dtype=np.float32)
        I32p = np.concatenate([np.eye(32), np.zeros((32, 32))], 1)
        I32q = np.concatenate([np.zeros((32, 32)), np.eye(32)], 1)
        sel8 = np.zeros((8, 128), np.float32)
        for r in range(8):
            sel8[r, 64 * (r % 2):64 * (r % 2) + 64] = 1
        psel8 = np.zeros((8, 128), np.float32)
        for r in range(8):
            if r % 2 == 0:
                psel8[r, 0:32] = 1; psel8[r, 64:96] = 1
            else:
                psel8[r, 32:64] = 1; psel8[r, 96:128] = 1
        cn = dict(
            cI64=np.tile(eye64, (2, 1)).astype(BF),
            cIC1=np.tile(I32p, (4, 1)).astype(BF),
            cIC2=np.tile(I32q, (4, 1)).astype(BF),
            cUTlo=np.concatenate([np.triu(np.ones((64, 64))),
                                  np.zeros((64, 64))], 0).astype(BF),
            cUThi=np.concatenate([np.zeros((64, 64)),
                                  np.triu(np.ones((64, 64)))], 0).astype(BF),
            cMASK=np.tile(np.where(np.tril(np.ones((C, C)), -1).T > 0, 0.0,
                                   -1e30).astype(np.float32), (1, NCH)).astype(BF),
            cDUP64=np.tile(np.eye(64), (1, 2)).astype(BF),
            cSWAP=_swap32().astype(BF),
            cSEL8=sel8.astype(BF),
            cPSEL8=psel8.astype(BF),
            cONESC=np.ones((128, 1), np.float32).astype(BF),
            cONESR=np.ones((1, 128), np.float32).astype(BF),
            cONES8=np.ones((8, 1), np.float32).astype(BF),
            cI128=np.eye(128, dtype=np.float32).astype(BF),
        )
        for p in range(NP):
            m = np.zeros((128, 8), np.float32)
            m[0:64, 2 * p] = 1
            m[64:128, 2 * p + 1] = 1
            cn[f"cMS{p}"] = m.astype(BF)
            selp = np.zeros((8, 128), np.float32)
            for mm_ in range(128):
                selp[2 * p + (mm_ // 64), mm_] = 1
            cn[f"cSELP{p}"] = selp.astype(BF)
            pselp = np.zeros((8, 128), np.float32)
            for mm_ in range(128):
                pselp[2 * p + ((mm_ // 32) % 2), mm_] = 1
            cn[f"cPSELP{p}"] = pselp.astype(BF)
        self.consts_np = cn
        for fn_ in ("cI128", "cUTlo", "cUThi", "cDUP64", "cMASK", "cSELP0",
                    "cSELP1", "cSELP2", "cSELP3"):
            cn[fn_ + "f"] = cn[fn_].astype(np.float32)
        for dead in ("cUTlo", "cUThi", "cDUP64", "cMASK", "cIC1", "cIC2",
                     "cSEL8", "cPSEL8", "cPSELP0", "cPSELP1", "cPSELP2",
                     "cPSELP3", "cSELP0", "cSELP1", "cSELP2", "cSELP3",
                     "cSWAP" if False else "cONESC" if False else "cIC1"):
            cn.pop(dead, None)
        self.cap = {n: nc.inline_tensor(a, "ic_" + n).ap() for n, a in cn.items()}

        self.S = {n: es.enter_context(nc.semaphore("s_" + n))
                  for n in ("ld", "gld", "tp", "st", "T", "V", "A", "G", "SY")}
        self.PB = [es.enter_context(nc.psum_tensor(f"PB{i}", [128, 512], F32))
                   for i in range(8)]
        sb = lambda n, s, d=BF16: es.enter_context(nc.sbuf_tensor(n, s, d))
        self.cs = {n: sb("s_" + n, list(a.shape),
                         F32 if n.endswith("f") else BF16)
                   for n, a in cn.items()}

        self.cbias = sb("cbias", [128, 4], F32)
        for j, val in enumerate((1e-5, 1e-6, 1e-12)):
            nc.const_aps.register(F32, val, self.cbias[:, j:j + 1])

        self.xtok = [sb(f"xtok{i}", [128, HID], F32)
                     for i in range(max(self.NTW, 2))]
        self.xsq = sb("xsq", [128, HID], F32)
        self.xhatt = [sb(f"xhatt{i}", [128, HID]) for i in range(max(self.NTW, 2))]
        self.statc = sb("statc", [128, 16], F32)

        if "no_attn" not in self.debug:
            self._attn_phase()
        if "no_xf" not in self.debug:
            nc.all_core_barrier()
            self._x_phase()
            if "no_f" not in self.debug:
                nc.all_core_barrier()
                self._f_phase()
        es.close()
        return nc

    def _attn_phase(self):
        nc = self.nc
        T, W, NW, NCH, NTW = self.T, self.W, self.NW, self.NCH, self.NTW
        S, PB, cs = self.S, self.PB, self.cs
        ats = ExitStack()
        sb = lambda n, s, d=BF16: ats.enter_context(nc.sbuf_tensor(n, s, d))
        b = self.b = {}

        b["wa_s"] = sb("wa_s", [128, 8 * 2064])
        b["wo_s"] = sb("wo_s", [128, 4 * 1024])
        b["cw_s"] = sb("cw_s", [128, 48], F32)
        b["cb_s"] = sb("cb_s", [128, 12], F32)
        b["csa_s"] = sb("csa_s", [128, 2 * W])
        b["csb_s"] = sb("csb_s", [128, 2 * W])
        for n in self.small_in:
            b[n] = sb(n + "_s", [8, 1], F32)
        b["onorm_s"] = sb("onorm_s", [1, 64], F32)
        b["onormR"] = sb("onormR", [1, 512])
        b["abcolB"] = sb("abcolB", [8, 1], F32)
        b["onormT"] = sb("onormT", [128, 512], F32)
        b["negexpA"] = sb("negexpA", [8, 1], F32)
        b["abcol"] = sb("abcol", [8, 1], F32)

        b["xhf"] = sb("xhf", [128, 8 * W])
        for nm, n in (("qpre", NP), ("kpre", NP), ("vpre", NP)):
            b[nm] = [sb(f"{nm}{p}", [128, W + 3]) for p in range(n)]
        b["qcv"] = sb("qcv", [128, W])
        b["kcv"] = sb("kcv", [128, W])
        b["q2"] = [sb(f"q2_{p}", [128, W]) for p in range(NP)]
        b["k2"] = [sb(f"k2_{p}", [128, W]) for p in range(NP)]
        b["vfeat"] = [[sb(f"vfeat{j}_{p}", [128, W]) for p in range(NP)]
                      for j in range(2)]
        b["gfeat"] = [[sb(f"gfeat{j}_{p}", [128, W]) for p in range(NP)]
                      for j in range(2)]
        b["negk2"] = [sb(f"negk2_{q}", [128, W]) for q in range(2)]
        b["k2r"] = [sb(f"k2r{q}", [128, W]) for q in range(2)]
        b["k2s"] = [sb(f"k2s{q}", [128, W]) for q in range(2)]
        b["qt"] = [[sb(f"qt{j}_{p}", [128, W]) for p in range(NP)]
                   for j in range(2)]
        b["cnvt"] = [sb(f"cnvt{j}", [128, W], F32) for j in range(2)]
        b["rop1"] = sb("rop1", [128, W], F32)
        b["rop2"] = sb("rop2", [128, W], F32)
        b["sqt"] = [sb(f"sqt{j}", [128, W]) for j in range(3)]
        b["abzA"] = sb("abzA", [8, W], F32)
        b["abzB"] = sb("abzB", [8, W], F32)
        b["rscr"] = sb("rscr", [8, W], F32)
        b["rscr2"] = sb("rscr2", [8, W], F32)
        b["rscrB"] = sb("rscrB", [8, W], F32)
        b["rscr2B"] = sb("rscr2B", [8, W], F32)
        for n in ("la", "lnb", "LLk", "LLq", "g", "gprev", "rowKt",
                  "rowQt", "bj", "sK", "qks", "coef"):
            b["r_" + n] = sb("r_" + n, [8, W], F32)
        b["rb_coef"] = [sb(f"rb_coef{j}", [8, W]) for j in range(2)]
        b["gCrow"] = sb("gCrow", [8, NCH], F32)

        b["latok"] = sb("latok", [128, NTW * 8], F32)
        b["gtokb"] = sb("gtokb", [128, NTW * 8], F32)
        b["bjt"] = [sb(f"bjt{p}", [128, NCH], F32) for p in range(NP)]
        b["gCcol"] = [sb(f"gCcol{j}", [128, NCH * NP], F32) for j in range(2)]
        b["exb"] = [sb(f"exb{q}", [128, W], F32) for q in range(2)]
        b["Etmp"] = [sb(f"Etmp{q}", [128, W], F32) for q in range(2)]
        b["Eb"] = [sb(f"Eb{q}", [128, W]) for q in range(2)]
        b["NT"] = [sb(f"NT{p}", [128, W]) for p in range(NP)]
        b["LT"] = [[sb(f"LT{j}_{p}", [128, W]) for p in range(NP)]
                   for j in range(2)]
        b["ladT"] = [[sb(f"ladT{q}_{s}", [128, W]) for s in range(5)]
                     for q in range(2)]
        b["ladU"] = [[sb(f"ladU{q}_{s}", [128, W]) for s in range(4)]
                     for q in range(2)]
        b["NU"] = [b["ladU"][q][3] for q in range(2)]
        b["X"] = [sb(f"X{j}", [128, NP * NCH * 128]) for j in range(2)]
        b["Y1T"] = [[sb(f"Y1T{j}_{p}", [128, NCH * 64]) for p in range(NP)]
                    for j in range(2)]
        b["KT"] = [[sb(f"KT{j}_{p}", [128, NCH * 64]) for p in range(NP)]
                   for j in range(2)]
        b["Wcb"] = [sb(f"Wcb{q}", [128, NP * 64]) for q in range(2)]
        b["Sst"] = sb("Sst", [128, NP * 64], F32)
        b["Ssb"] = [sb(f"Ssb{q}", [128, NP * 64]) for q in range(2)]
        self.gch = 0
        b["otok"] = [sb(f"otok{i}", [128, 512], F32) for i in range(NTW)]
        b["vtok"] = [sb(f"vtok{i}", [128, 512]) for i in range(NTW)]
        b["gtokT"] = sb("gtokT", [128, NTW * 512])
        b["coefcol"] = sb("coefcol", [128, NTW * 8], F32)
        b["orst"] = sb("orst", [128, 8], F32)
        b["on2"] = sb("on2", [128, 512], F32)
        b["osil"] = sb("osil", [128, 512])
        b["o2f"] = sb("o2f", [128, 512])
        b["o2ff"] = sb("o2ff", [128, 512])
        self.GW = min(4, self.NW)
        b["pstage"] = sb("pstage", [128, self.GW * NTW * 1024], F32)

        with nc.Block() as blk:
            self.blk = blk
            t_wg = self.dma("g", [], lambda gp: [
                gp.dma_start(out=b["wa_s"][:].rearrange("p (a b) -> p a b", a=8),
                             in_=self.wa_in.rearrange("a p b -> p a b")),
                gp.dma_start(out=b["wo_s"][:].rearrange("p (a b) -> p a b", a=4),
                             in_=self.wo_in.rearrange("a p b -> p a b"))]
                + [gp.dma_start(out=b[n][:], in_=self.small_in[n])
                   for n in self.small_in]
                + [gp.dma_start(out=b["onorm_s"][:], in_=self.onorm_in)])
            t_ws = self.dma("s", [], lambda sy: [
                sy.dma_start(out=self.cs[n][:], in_=self.cap[n])
                for n in self.consts_np]
                + [sy.dma_start(out=b["cw_s"][:], in_=self.cw_in),
                   sy.dma_start(out=b["cb_s"][:], in_=self.cb_in)])
            self.wld = self.tok(S["ld"])
            self.wldg = t_wg

            self.dynv = {}
            def _snap_init(sy):
                pid = sy.partition_id()
                self.dynv["oslot"] = sy.snap(1 - pid % 2, min_val=0, max_val=1)
            self.blk.sync(_snap_init)
            self.war_cs = {}

            tA = self.step("a", [self.wld, self.wldg], lambda a:
                           a.activation(b["negexpA"][:], b["alog8"][:], AF.Exp))
            tV = self.step("v", [self.wld, self.wldg, tA], lambda v: (
                v.memset(self.cbias[:, 0:1], 1e-5),
                v.memset(self.cbias[:, 1:2], 1e-6),
                v.memset(self.cbias[:, 2:3], 1e-12),
                v.tensor_scalar_mul(b["negexpA"][:], b["negexpA"][:], -1.0),
                v.tensor_add(b["abcol"][0:8, :], b["bgk8"][:], b["dtb8"][:]),
                v.tensor_copy(b["abcolB"][:], b["bb8"][:]),
                v.tensor_copy(b["onormR"][0:1, :].rearrange("p (a b) -> p a b", a=8),
                              b["onorm_s"][:].rearrange("p (a b) -> p a b", a=1)
                              .broadcast_to([1, 8, 64])),
                [v.memset(b[nm][p][:, 0:3], 0.0)
                 for nm in ("qpre", "kpre", "vpre") for p in range(NP)],
                v.memset(b["Sst"][:], 0.0),
                v.memset(b["Ssb"][0][:], 0.0),
                v.memset(b["Ssb"][1][:], 0.0))[-1])
            tT = self.step("t", [tV], lambda t:
                           t.matmul(PB[0][:, 0:512], cs["cONESR"][:],
                                    b["onormR"][0:1, :], start=True, stop=True))
            tV2 = self.step("v", [tT], lambda v:
                            v.tensor_copy(b["onormT"][:], PB[0][:, 0:512]))
            self.pb_free[0] = tV2

            self.war_xtok = (S["A"], 0)
            self.war_xhf = (S["T"], 0)
            self.war_win = (S["V"], 0)
            for w in range(NW):
                self._window(w)
            self._chain_drain()

            self.blk.gpsimd(lambda gp: gp.wait_ge(S["st"], self.cnt.get(S["st"].name, 0)))
        ats.close()

    def _window(self, w):
        S, PB, cs, b = self.S, self.PB, self.cs, self.b
        T, W, NCH, NTW = self.T, self.W, self.NCH, self.NTW
        wo = w * W
        wp = w % 2
        st = self.step
        xtok, xsq, xhatt, statc = self.xtok, self.xsq, self.xhatt, self.statc

        t_cs = self.dma("s", [self.war_cs.get(w % 2, (S["V"], 0))], lambda sy: [
            sy.dma_start(out=b["csa_s"][:, (w % 2) * W:(w % 2 + 1) * W],
                         in_=self.csa_in[:, wo:wo + W]),
            sy.dma_start(out=b["csb_s"][:, (w % 2) * W:(w % 2 + 1) * W],
                         in_=self.csb_in[:, wo:wo + W])])
        self.t_cs = t_cs
        t_x = self.dma("s", [self.war_xtok], lambda sy: [
            sy.dma_start(out=xtok[i][:],
                         in_=self.x_in[wo + 128 * i: wo + 128 * (i + 1), :])
            for i in range(NTW)])
        t_app = None
        for i in range(NTW):
            c = statc
            tv = st("v", [t_x], lambda v, i=i: (
                v.tensor_mul(self.xsq[:], xtok[i][:], xtok[i][:]),
                v.reduce_sum(c[:, 4 * i:4 * i + 1], xtok[i][:], AX.X),
                v.reduce_sum(c[:, 4 * i + 1:4 * i + 2], self.xsq[:],
                             AX.X),
                v.tensor_scalar_mul(c[:, 4 * i:4 * i + 1],
                                    c[:, 4 * i:4 * i + 1], 1.0 / HID),
                v.tensor_scalar_mul(c[:, 4 * i + 1:4 * i + 2],
                                    c[:, 4 * i + 1:4 * i + 2], 1.0 / HID),
                v.scalar_tensor_tensor(c[:, 4 * i + 2:4 * i + 3],
                                       c[:, 4 * i:4 * i + 1],
                                       c[:, 4 * i:4 * i + 1],
                                       c[:, 4 * i + 1:4 * i + 2],
                                       ALU.mult, ALU.subtract))[-1])
            ta_ = st("a", [tv], lambda a, i=i:
                     a.activation(c[:, 4 * i + 3:4 * i + 4],
                                  c[:, 4 * i + 2:4 * i + 3], AF.Sqrt,
                                  bias=1e-5, scale=-1.0))
            t_app = st("v", [ta_, getattr(self, "war_xhatt", (S["tp"], 0))],
                       lambda v, i=i: (
                       v.reciprocal(c[:, 4 * i + 3:4 * i + 4],
                                    c[:, 4 * i + 3:4 * i + 4]),
                       v.tensor_scalar(xhatt[i][:], xtok[i][:],
                                       c[:, 4 * i:4 * i + 1],
                                       c[:, 4 * i + 3:4 * i + 4],
                                       ALU.subtract, ALU.mult))[-1])
        self.war_xtok = t_app

        t_tp = self.step("s", [t_app, self.war_xhf, self.tok(S["tp"])], lambda sy: [
            self._tp(sy, b["xhf"][:].rearrange("p (c t) -> p c t", c=8)
                     [:, :, 128 * i:128 * (i + 1)], xhatt[i][:])
            for i in range(NTW)][-1] and None, dma=True)
        t_tp = self.tok(S["tp"])
        self.war_xhatt = t_tp
        self._chain_step(3)
        if "s1" in self.debug:
            return

        dsts = ([("qkv", b["qpre"][p]) for p in range(NP)]
                + [("qkv", b["kpre"][p]) for p in range(NP)]
                + [("qkv", b["vpre"][p]) for p in range(NP)]
                + [("g", b["gfeat"][wp][p]) for p in range(NP)]
                + [("abA", b["abzA"]), ("abB", b["abzB"])])
        for ot in range(18):
            bk = 4 + ot % 2
            waits = [self.bank(bk)]
            if ot == 0:
                ww = self.war_win if isinstance(self.war_win, list) \
                    else [self.war_win]
                waits += [t_tp, self.wld, self.wldg] + ww
            M = 128 if ot < 16 else 8
            co = 2048 if ot < 16 else (2048 + 8 * (ot - 16))
            tmm = st("t", waits, lambda t, ot=ot, bk=bk, M=M: [
                t.matmul(PB[bk][0:M, 0:W],
                         b["wa_s"][:, dc * 2064 + (ot * 128 if ot < 16 else
                                                   2048 + 8 * (ot - 16)):
                                   dc * 2064 + (ot * 128 if ot < 16 else
                                                2048 + 8 * (ot - 16)) + M],
                         b["xhf"][:, dc * W:(dc + 1) * W],
                         start=(dc == 0), stop=(dc == 7))
                for dc in range(8)][-1])
            kind, dst = dsts[ot]
            tev = st("v", [tmm], lambda v, kind=kind, dst=dst, bk=bk, M=M:
                     v.tensor_copy(dst[:, 3:3 + W] if kind == "qkv"
                                   else dst[0:M, 0:W],
                                   PB[bk][0:M, 0:W]))
            self.pb_free[bk] = tev
        t_proj = tev
        self.war_xhf = self.tok(S["T"])
        self._chain_step(3)
        if "s2" in self.debug:
            return

        for p in range(NP):
            for qi in range(3):
                src = (b["qpre"], b["kpre"], b["vpre"])[qi][p]
                dst = (b["qcv"], b["kcv"], b["vfeat"][wp][p])[qi]
                base = qi * 16 + p * 4
                cj = (qi * NP + p) % 2
                wcn = getattr(self, "war_cnvt", {})
                self.war_cnvt = wcn
                tc = st("v", [t_proj, wcn.get(cj, (S["A"], 0))],
                        lambda v, src=src, base=base, cj=cj: (
                    v.tensor_scalar_mul(b["cnvt"][cj][:, 0:W], src[:, 0:W],
                                        b["cw_s"][:, base:base + 1]),
                    [v.scalar_tensor_tensor(b["cnvt"][cj][:, 0:W],
                                            src[:, tap:tap + W],
                                            b["cw_s"][:, base + tap:base + tap + 1],
                                            b["cnvt"][cj][:, 0:W],
                                            ALU.mult, ALU.add)
                     for tap in (1, 2, 3)],
                    v.tensor_copy(src[:, 0:3], src[:, W:W + 3]))[-1])
                war_cv = getattr(self, "war_cv", {})
                tsl = st("a", [tc, war_cv.get(qi, (S["T"], 0))],
                         lambda a, qi=qi, p=p, dst=dst, cj=cj:
                         a.activation(dst[:, 0:W], b["cnvt"][cj][:, 0:W],
                                      AF.Silu,
                                      bias=b["cb_s"][:, qi * 4 + p:qi * 4 + p + 1]))
                wcn[cj] = tsl
                if qi < 2:
                    rsrc = b["qcv"] if qi == 0 else b["kcv"]
                    rdst = (b["q2"] if qi == 0 else b["k2"])[p]
                    bk = 6
                    tsw = st("t", [tsl, self.bank(bk)], lambda t, rsrc=rsrc, bk=bk:
                             t.matmul(PB[bk][:, 0:W], cs["cSWAP"][:],
                                      rsrc[:, 0:W], start=True, stop=True))
                    cso = (w % 2) * W
                    trp = st("v", [tsw, self.t_cs],
                             lambda v, rsrc=rsrc, rdst=rdst, bk=bk, cso=cso: (
                        v.tensor_mul(b["rop1"][:, 0:W], PB[bk][:, 0:W],
                                     b["csb_s"][:, cso:cso + W]),
                        v.tensor_mul(b["rop2"][:, 0:W], rsrc[:, 0:W],
                                     b["csa_s"][:, cso:cso + W]),
                        v.tensor_add(rdst[:, 0:W], b["rop2"][:, 0:W],
                                     b["rop1"][:, 0:W]))[-1])
                    self.pb_free[bk] = trp
                    war_cv[qi] = trp
                    self.war_cv = war_cv
        t_qkv = self.tok(S["V"])
        t_silu = self.tok(S["A"])
        self.war_cs[w % 2] = t_qkv
        self._chain_step(2)
        if "s4" in self.debug:
            return

        r = lambda n: b["r_" + n]
        rb = lambda n: b["rb_" + n]
        t1 = st("v", [t_proj], lambda v: (
            v.tensor_scalar_add(b["rscr"][:, 0:W], b["abzA"][:, 0:W],
                                b["abcol"][:, 0:1]),
            v.tensor_scalar_add(b["rscrB"][:, 0:W], b["abzB"][:, 0:W],
                                b["abcolB"][:, 0:1]))[-1])
        t2 = st("a", [t1], lambda a: (
            a.activation(b["rscr2"][:, 0:W], b["rscr"][:, 0:W], AF.Abs),
            a.activation(b["rscr2B"][:, 0:W], b["rscrB"][:, 0:W], AF.Abs),
            a.activation(b["rscr2"][:, 0:W], b["rscr2"][:, 0:W], AF.Exp,
                         scale=-1.0),
            a.activation(b["rscr2B"][:, 0:W], b["rscr2B"][:, 0:W], AF.Exp,
                         scale=-1.0),
            a.activation(b["rscr2"][:, 0:W], b["rscr2"][:, 0:W], AF.Ln,
                         bias=1.0),
            a.activation(b["rscr2B"][:, 0:W], b["rscr2B"][:, 0:W], AF.Ln,
                         bias=1.0))[-1])
        t3a = st("v", [t2], lambda v: (
            v.tensor_scalar_max(r("la")[:, 0:W], b["rscr"][:, 0:W], 0.0),
            v.tensor_add(r("la")[:, 0:W], r("la")[:, 0:W], b["rscr2"][:, 0:W]),
            v.tensor_scalar_mul(r("la")[:, 0:W], r("la")[:, 0:W],
                                b["negexpA"][:, 0:1]))[-1])
        t3 = st("v", [t2], lambda v: (
            v.tensor_scalar_min(r("lnb")[:, 0:W], b["rscrB"][:, 0:W], 0.0),
            v.tensor_sub(r("lnb")[:, 0:W], r("lnb")[:, 0:W],
                         b["rscr2B"][:, 0:W]))[-1])

        t_sums = []
        sums = (("LLk", "k2", "k2"), ("LLq", "q2", "q2"), ("qks", "q2", "k2"))
        tsqd = {}
        tmm = {j: None for j in range(3)}
        for p in range(NP):
            for j, (nm, aa, bb_) in enumerate(sums):
                tsqd[(j, p)] = st("v", [t_qkv]
                                  + ([tmm[j]] if tmm[j] else []),
                                  lambda v, nm=nm, aa=aa, bb_=bb_, p=p, j=j:
                                  v.tensor_mul(b["sqt"][j][:, 0:W],
                                               b[aa][p][:, 0:W],
                                               b[bb_][p][:, 0:W]))
            for j in range(3):
                tmm[j] = st("t", [tsqd[(j, p)]]
                            + ([self.bank(4 + j)] if p == 0 else []),
                            lambda t, p=p, j=j:
                            t.matmul(PB[4 + j][0:8, 0:W], cs[f"cMS{p}"][:],
                                     b["sqt"][j][:, 0:W], start=(p == 0),
                                     stop=(p == 3)))
        for j, (nm, aa, bb_) in enumerate(sums):
            if nm == "qks":
                tl = st("v", [tmm[j]], lambda v, nm=nm, j=j:
                        v.tensor_copy(r(nm)[:, 0:W], PB[4 + j][0:8, 0:W]))
            else:
                tl = st("a", [tmm[j]], lambda a, nm=nm, j=j:
                        a.activation(r(nm)[:, 0:W], PB[4 + j][0:8, 0:W],
                                     AF.Ln, bias=1e-12))
            self.pb_free[4 + j] = tl
            t_sums.append(tl)

        t4 = st("t", [t3, t3a, self.bank(7)], lambda t: [
            t.matmul(PB[7][:, 8 * i:8 * i + 8],
                     r("la")[:, 128 * i:128 * (i + 1)],
                     cs["cI128f"][0:8, 0:8], start=(i == 0), stop=(i == NTW - 1))
            for i in range(NTW)][-1])
        t5 = st("v", [t4], lambda v:
                v.tensor_copy(b["latok"][:, 0:NTW * 8], PB[7][:, 0:NTW * 8]))
        self.pb_free[7] = t5
        t6 = st("t", [t5, self.bank(7)], lambda t: [
            t.matmul(PB[7][64 * u:64 * u + 64, 8 * i:8 * i + 8],
                     cs["cUTlof" if u == 0 else "cUThif"][:],
                     b["latok"][:, 8 * i:8 * i + 8],
                     start=(i == 0), stop=(i == NTW - 1),
                     tile_position=(0, 64 * u))
            for u in range(2) for i in range(NTW)][-1])
        t7 = st("v", [t6], lambda v:
                v.tensor_copy(b["gtokb"][:, 0:NTW * 8], PB[7][:, 0:NTW * 8]))
        self.pb_free[7] = t7
        t8 = st("t", [t7, self.bank(7)], lambda t: [
            t.matmul(PB[7][0:8, 128 * i:128 * (i + 1)],
                     b["gtokb"][:, 8 * i:8 * i + 8], cs["cI128f"][:],
                     start=(i == 0), stop=(i == NTW - 1))
            for i in range(NTW)][-1])
        t9 = st("v", [t8] + t_sums, lambda v: (
            v.tensor_copy(r("g")[:, 0:W], PB[7][0:8, 0:W]),
            v.tensor_sub(r("gprev")[:, 0:W], r("g")[:, 0:W], r("la")[:, 0:W]),
            v.tensor_copy(b["gCrow"][:].rearrange("p (c o) -> p c o", o=1),
                          r("g")[:, 0:W].rearrange("p (c t) -> p c t", t=64)
                          [:, :, 63:64]),

            v.scalar_tensor_tensor(r("rowKt")[:, 0:W], r("LLk")[:, 0:W], -0.5,
                                   r("g")[:, 0:W], ALU.mult, ALU.add),
            v.scalar_tensor_tensor(r("rowQt")[:, 0:W], r("LLq")[:, 0:W], -0.5,
                                   r("gprev")[:, 0:W], ALU.mult, ALU.add),
            v.scalar_tensor_tensor(r("bj")[:, 0:W], r("g")[:, 0:W], -1.0,
                                   r("lnb")[:, 0:W], ALU.mult, ALU.add),
            v.scalar_tensor_tensor(r("bj")[:, 0:W], r("LLk")[:, 0:W], -0.5,
                                   r("bj")[:, 0:W], ALU.mult, ALU.add),
            v.tensor_add(r("sK")[:, 0:W].rearrange("p (c t) -> p c t", t=64),
                         r("bj")[:, 0:W].rearrange("p (c t) -> p c t", t=64),
                         b["gCrow"][:].rearrange("p (c o) -> p c o", o=1)
                         .broadcast_to([8, NCH, 64])),
            v.tensor_add(r("coef")[:, 0:W], r("LLq")[:, 0:W], r("LLk")[:, 0:W]),
            v.tensor_scalar_mul(r("coef")[:, 0:W], r("coef")[:, 0:W], -0.5))[-1])
        self.pb_free[7] = t9
        t10 = st("a", [t9], lambda a:
                 a.activation(b["rscr2"][:, 0:W], r("coef")[:, 0:W], AF.Exp))
        t11 = st("v", [t10], lambda v: (
            v.tensor_mul(r("coef")[:, 0:W], r("qks")[:, 0:W], b["rscr2"][:, 0:W]),
            v.tensor_scalar_mul(r("coef")[:, 0:W], r("coef")[:, 0:W],
                                b["dp8"][:, 0:1]),
            v.tensor_copy(b["rb_coef"][w % 2][:, 0:W], r("coef")[:, 0:W]))[-1])

        t12 = st("t", [t11, self.bank(5)], lambda t: [
            t.matmul(PB[5][64 * u:64 * u + 64, p * NCH + cc:p * NCH + cc + 1],
                     r("bj")[0:8, cc * 64:(cc + 1) * 64],
                     cs["cI128f"][0:8, 2 * p + u:2 * p + u + 1],
                     start=(p == 0 and cc == 0),
                     stop=(p == NP - 1 and cc == NCH - 1),
                     tile_position=(0, 64 * u))
            for u in range(2) for p in range(NP) for cc in range(NCH)][-1])
        t13 = st("v", [t12], lambda v: [
            v.tensor_copy(b["bjt"][p][:], PB[5][:, p * NCH:(p + 1) * NCH])
            for p in range(NP)][-1])
        self.pb_free[5] = t13
        t14 = st("t", [t11, self.bank(6)], lambda t: [
            t.matmul(PB[6][:, cc * NP + p:cc * NP + p + 1],
                     cs[f"cSELP{p}f"][:],
                     b["gCrow"][0:8, cc:cc + 1],
                     start=(cc == 0 and p == 0),
                     stop=(cc == NCH - 1 and p == NP - 1))
            for cc in range(NCH) for p in range(NP)][-1])
        t15 = st("a", [t14], lambda a:
                 a.activation(b["gCcol"][wp][:, 0:NCH * NP],
                              PB[6][:, 0:NCH * NP], AF.Exp))
        self.pb_free[6] = t15
        self._chain_step(1)
        if "s5" in self.debug:
            return

        BT = (4, 6)
        BU = (5, 7)
        pair_tok = [None] * NP
        wx = getattr(self, "war_px", {})
        self.war_px = wx
        NRND = (NCH + 3) // 4
        for gdx in range(NP // 2):
            ps = (2 * gdx, 2 * gdx + 1)
            Q = {p: p % 2 for p in ps}
            wq = {p: wx.get(Q[p], (S["V"], 0)) for p in ps}

            tks = {p: [] for p in ps}
            for rowname, dname in (("rowKt", "k2r"), ("sK", "k2s"),
                                   ("rowQt", "qt")):
                tbm = {}
                for p in ps:
                    q = Q[p]
                    tbm[p] = st("t", [t9, self.bank(BU[q])],
                                lambda t, rowname=rowname, p=p, q=q:
                        t.matmul(PB[BU[q]][:, 0:W], cs[f"cSELP{p}f"][:],
                                 r(rowname)[0:8, 0:W], start=True, stop=True))
                tex = {}
                for p in ps:
                    q = Q[p]
                    tex[p] = st("a", [tbm[p], wq[p],
                                      wx.get(("exb", q), (S["V"], 0))],
                                lambda a, q=q:
                                a.activation(b["exb"][q][:, 0:W],
                                             PB[BU[q]][:, 0:W], AF.Exp))
                    self.pb_free[BU[q]] = tex[p]
                for p in ps:
                    q = Q[p]
                    dst = b["qt"][wp][p] if dname == "qt" else b[dname][q]
                    srcs = "q2" if rowname == "rowQt" else "k2"
                    tm = st("v", [tex[p], t_qkv, wq[p]],
                            lambda v, dst=dst, srcs=srcs, p=p, q=q:
                            v.tensor_mul(dst[:, 0:W], b[srcs][p][:, 0:W],
                                         b["exb"][q][:, 0:W]))
                    wx[("exb", q)] = tm
                    tks[p].append(tm)
            t_neg = {}
            for p in ps:
                q = Q[p]
                t_neg[p] = st("v", [t_qkv, wq[p]],
                              lambda v, p=p, q=q:
                              v.tensor_scalar_mul(b["negk2"][q][:, 0:W],
                                                  b["k2"][p][:, 0:W], -1.0))

            t_gram = {}
            t_nb = {p: None for p in ps}
            for rowname, dstl, grhs in (("rowKt", "NT", "k2"),
                                        ("rowQt", "LT", "q2")):
                for p in ps:
                    q = Q[p]
                    t_gram[p] = st("t", [t_neg[p], self.bank(BT[q])]
                                   + ([t_nb[p]] if t_nb[p] else []),
                                   lambda t, p=p, q=q, grhs=grhs: [
                        t.matmul(PB[BT[q]][64 * u:64 * u + 64,
                                           cc * 64:(cc + 1) * 64],
                                 b["negk2"][q][64 * u:64 * u + 64,
                                               cc * 64:(cc + 1) * 64],
                                 b[grhs][p][64 * u:64 * u + 64,
                                            cc * 64:(cc + 1) * 64],
                                 start=(cc == 0), stop=(cc == NCH - 1),
                                 tile_position=(64 * u, 64 * u))
                        for u in range(2) for cc in range(NCH)][-1])
                ta_ = {}
                for p in ps:
                    q = Q[p]
                    ta_[p] = st("t", [t9, self.bank(BU[q])],
                                lambda t, rowname=rowname, p=p, q=q: (
                        t.matmul(PB[BU[q]][:, 0:W], cs[f"cSELP{p}f"][:],
                                 r(rowname)[0:8, 0:W], start=True, stop=False),
                        t.matmul(PB[BU[q]][:, 0:W], cs["cDUP64f"][:],
                                 cs["cMASKf"][:, 0:W],
                                 start=False, stop=True))[-1])
                tt1 = {}
                for p in ps:
                    q = Q[p]
                    tt1[p] = st("v", [ta_[p], wq[p],
                                      wx.get(("Etmp", q), (S["A"], 0))],
                                lambda v, p=p, q=q:
                                v.tensor_add(b["Etmp"][q][:, 0:W].rearrange(
                                    "p (c t) -> p c t", t=64),
                                    PB[BU[q]][:, 0:W].rearrange(
                                        "p (c t) -> p c t", t=64),
                                    b["bjt"][p][:].rearrange(
                                        "p (c o) -> p c o", o=1)
                                    .broadcast_to([128, NCH, 64])))
                    self.pb_free[BU[q]] = tt1[p]
                texp = {}
                for p in ps:
                    q = Q[p]
                    texp[p] = st("a", [tt1[p], wx.get(("Eb", q), (S["V"], 0))],
                                 lambda a, q=q:
                                 a.activation(b["Eb"][q][:, 0:W],
                                              b["Etmp"][q][:, 0:W], AF.Exp))
                    wx[("Etmp", q)] = texp[p]
                for p in ps:
                    q = Q[p]
                    t_nb[p] = st("v", [texp[p], t_gram[p]],
                                 lambda v, dstl=dstl, p=p, q=q:
                                 v.tensor_mul(
                                     (b["LT"][wp] if dstl == "LT"
                                      else b["NT"])[p][:, 0:W],
                                     b["Eb"][q][:, 0:W],
                                     PB[BT[q]][:, 0:W]))
                    wx[("Eb", q)] = t_nb[p]
                    self.pb_free[BT[q]] = t_nb[p]

            t_nu = {}
            for p in ps:
                q = Q[p]
                tmm = st("t", [t_nb[p], self.bank(BT[q])],
                         lambda t, p=p, q=q: [
                    t.matmul(PB[BT[q]][64 * u:64 * u + 64,
                                       cc * 64:(cc + 1) * 64],
                             b["NT"][p][64 * u:64 * u + 64,
                                        cc * 64:(cc + 1) * 64],
                             cs["cI64"][64 * u:64 * u + 64, :],
                             start=(cc == 0), stop=(cc == NCH - 1),
                             tile_position=(64 * u, 64 * u))
                    for u in range(2) for cc in range(NCH)][-1])
                t_nu[p] = st("v", [tmm, wq[p]], lambda v, q=q:
                             v.tensor_copy(b["NU"][q][:, 0:W],
                                           PB[BT[q]][:, 0:W]))
                self.pb_free[BT[q]] = t_nu[p]

            curT = {p: b["NT"][p] for p in ps}
            curU = {p: b["NU"][Q[p]] for p in ps}
            tTprev = {p: t_nb[p] for p in ps}
            tUprev = dict(t_nu)
            for s in range(1, 6):
                tlt = {}
                for p in ps:
                    q = Q[p]
                    tlt[p] = st("t", [tTprev[p], tUprev[p], self.bank(BT[q])],
                                lambda t, cT=curT[p], cU=curU[p], q=q: [
                        t.matmul(PB[BT[q]][64 * u:64 * u + 64,
                                           cc * 64:(cc + 1) * 64],
                                 cU[64 * u:64 * u + 64, cc * 64:(cc + 1) * 64],
                                 cT[64 * u:64 * u + 64, cc * 64:(cc + 1) * 64],
                                 start=(cc == 0), stop=(cc == NCH - 1),
                                 tile_position=(64 * u, 64 * u))
                        for u in range(2) for cc in range(NCH)][-1])
                tlu = {}
                if s <= 4:
                    for p in ps:
                        q = Q[p]
                        tlu[p] = st("t", [tTprev[p], tUprev[p],
                                          self.bank(BU[q])],
                                    lambda t, cT=curT[p], cU=curU[p], q=q: [
                            t.matmul(PB[BU[q]][64 * u:64 * u + 64,
                                               cc * 64:(cc + 1) * 64],
                                     cT[64 * u:64 * u + 64,
                                        cc * 64:(cc + 1) * 64],
                                     cU[64 * u:64 * u + 64,
                                        cc * 64:(cc + 1) * 64],
                                     start=(cc == 0), stop=(cc == NCH - 1),
                                     tile_position=(64 * u, 64 * u))
                            for u in range(2) for cc in range(NCH)][-1])
                for p in ps:
                    q = Q[p]
                    tTe = st("a", [tlt[p], wq[p]], lambda a, s=s, q=q:
                             a.activation(b["ladT"][q][s - 1][:, 0:W],
                                          PB[BT[q]][:, 0:W], AF.Copy))
                    self.pb_free[BT[q]] = tTe
                    curT[p], tTprev[p] = b["ladT"][q][s - 1], tTe
                if s <= 4:
                    for p in ps:
                        q = Q[p]
                        tUe = st("v", [tlu[p], wq[p]], lambda v, s=s, q=q:
                                 v.tensor_copy(b["ladU"][q][s - 1][:, 0:W],
                                               PB[BU[q]][:, 0:W]))
                        self.pb_free[BU[q]] = tUe
                        curU[p], tUprev[p] = b["ladU"][q][s - 1], tUe

            t_x = {}
            for rnd in range(NRND):
                ccs = list(range(4 * rnd, min(4 * rnd + 4, NCH)))
                trh = {}
                for p in ps:
                    q = Q[p]
                    trh[p] = st("t", [tks[p][0], t_qkv, self.bank(BT[q])],
                                lambda t, ccs=ccs, p=p, q=q: [
                        mm for u in range(2) for cc in ccs for mm in (
                            t.matmul(PB[BT[q]][64 * u:64 * u + 64,
                                               (cc - ccs[0]) * 128:
                                               (cc - ccs[0]) * 128 + 64],
                                     b["k2r"][q][64 * u:64 * u + 64,
                                                 cc * 64:(cc + 1) * 64],
                                     cs["cI64"][64 * u:64 * u + 64, :],
                                     start=(cc == ccs[0]), stop=False,
                                     tile_position=(64 * u, 64 * u)),
                            t.matmul(PB[BT[q]][64 * u:64 * u + 64,
                                               (cc - ccs[0]) * 128 + 64:
                                               (cc - ccs[0]) * 128 + 128],
                                     b["vfeat"][wp][p][64 * u:64 * u + 64,
                                                       cc * 64:(cc + 1) * 64],
                                     cs["cI64"][64 * u:64 * u + 64, :],
                                     start=False, stop=(cc == ccs[-1]),
                                     tile_position=(64 * u, 64 * u)))][-1])
                for p in ps:
                    q = Q[p]
                    xo = p * NCH * 128
                    t_x[p] = st("v", [trh[p]], lambda v, ccs=ccs, xo=xo, q=q:
                                v.tensor_copy(
                                    b["X"][wp][:, xo + ccs[0] * 128:
                                               xo + (ccs[-1] + 1) * 128],
                                    PB[BT[q]][:, 0:len(ccs) * 128]))
                    self.pb_free[BT[q]] = t_x[p]

            for s in range(5, -1, -1):
                for rnd in range(NRND):
                    ccs = list(range(4 * rnd, min(4 * rnd + 4, NCH)))
                    tap = {}
                    for p in ps:
                        q = Q[p]
                        tiles = b["ladT"][q][s - 1] if s >= 1 else b["NT"][p]
                        xo = p * NCH * 128
                        tap[p] = st("t", [t_x[p], self.bank(BT[q])],
                                    lambda t, tiles=tiles, ccs=ccs, xo=xo,
                                    q=q: [
                            t.matmul(PB[BT[q]][64 * u:64 * u + 64,
                                               (cc - ccs[0]) * 128:
                                               (cc - ccs[0]) * 128 + 128],
                                     tiles[64 * u:64 * u + 64,
                                           cc * 64:(cc + 1) * 64],
                                     b["X"][wp][64 * u:64 * u + 64,
                                                xo + cc * 128:
                                                xo + cc * 128 + 128],
                                     start=(cc == ccs[0]), stop=(cc == ccs[-1]),
                                     tile_position=(64 * u, 64 * u))
                            for u in range(2) for cc in ccs][-1])
                    for p in ps:
                        q = Q[p]
                        xo = p * NCH * 128
                        t_x[p] = st("v", [tap[p]], lambda v, ccs=ccs, xo=xo,
                                    q=q:
                                    v.tensor_add(
                                        b["X"][wp][:, xo + ccs[0] * 128:
                                                   xo + (ccs[-1] + 1) * 128],
                                        b["X"][wp][:, xo + ccs[0] * 128:
                                                   xo + (ccs[-1] + 1) * 128],
                                        PB[BT[q]][:, 0:len(ccs) * 128]))
                        self.pb_free[BT[q]] = t_x[p]

            ty1 = {}
            tkt = {}
            for p in ps:
                q = Q[p]
                xo = p * NCH * 128
                tym = st("t", [t_x[p], self.bank(BT[q])],
                         lambda t, xo=xo, q=q: [
                    t.matmul(PB[BT[q]][64 * u:64 * u + 64,
                                       cc * 64:(cc + 1) * 64],
                             b["X"][wp][64 * u:64 * u + 64,
                                        xo + cc * 128: xo + cc * 128 + 64],
                             cs["cI64"][64 * u:64 * u + 64, :],
                             start=(cc == 0), stop=(cc == NCH - 1),
                             tile_position=(64 * u, 64 * u))
                    for u in range(2) for cc in range(NCH)][-1])
                ty1[p] = st("v", [tym, wq[p]], lambda v, p=p, q=q:
                            v.tensor_copy(b["Y1T"][wp][p][:, 0:NCH * 64],
                                          PB[BT[q]][:, 0:NCH * 64]))
                self.pb_free[BT[q]] = ty1[p]
            for p in ps:
                q = Q[p]
                tkm = st("t", [tks[p][1], self.bank(BU[q])],
                         lambda t, p=p, q=q: [
                    t.matmul(PB[BU[q]][64 * u:64 * u + 64,
                                       cc * 64:(cc + 1) * 64],
                             b["k2s"][q][64 * u:64 * u + 64,
                                         cc * 64:(cc + 1) * 64],
                             cs["cI64"][64 * u:64 * u + 64, :],
                             start=(cc == 0), stop=(cc == NCH - 1),
                             tile_position=(64 * u, 64 * u))
                    for u in range(2) for cc in range(NCH)][-1])
                tkt[p] = st("v", [tkm, wq[p]], lambda v, p=p, q=q:
                            v.tensor_copy(b["KT"][wp][p][:, 0:NCH * 64],
                                          PB[BU[q]][:, 0:NCH * 64]))
                self.pb_free[BU[q]] = tkt[p]
            for p in ps:
                pair_tok[p] = (t_x[p], ty1[p], tkt[p])
                wx[Q[p]] = tkt[p]
            self._chain_step(2)

        self._chain_drain()
        self.war_win = [self.tok(S["V"]), self.tok(S["T"]), self.tok(S["A"])]
        if "s6" not in self.debug:
            self.pending_chain = self._chain_gen(w, pair_tok, t_qkv, t11)

    def _chain_step(self, n=1):
        g = getattr(self, "pending_chain", None)
        if g is None:
            return
        for _ in range(n):
            try:
                next(g)
            except StopIteration:
                self.pending_chain = None
                return

    def _chain_drain(self):
        g = getattr(self, "pending_chain", None)
        if g is None:
            return
        for _ in g:
            pass
        self.pending_chain = None

    def _tp(self, sy, out_ap, in_ap):
        ins = sy.dma_start(out=out_ap, in_=in_ap, transpose=True)
        self._inc(ins, self.S["tp"], 16)
        return ins

    def _chain_gen(self, w, pair_tok, t_qkv, t11):
        S, PB, cs, b = self.S, self.PB, self.cs, self.b
        W, NCH, NTW = self.W, self.NCH, self.NTW
        wo = w * W
        wp = w % 2
        st = self.step

        for i in range(NTW):
            tvm = st("t", [t_qkv, self.bank(2)], lambda t, i=i: [
                t.matmul(PB[2][(cc % 2) * 64:(cc % 2) * 64 + 64,
                               p * 128 + u * 64: p * 128 + u * 64 + 64],
                         b["vfeat"][wp][p][64 * u:64 * u + 64,
                                           cc * 64:(cc + 1) * 64],
                         cs["cI64"][64 * u:64 * u + 64, :],
                         start=True, stop=True,
                         tile_position=(64 * u, (cc % 2) * 64))
                for p in range(NP) for u in range(2)
                for cc in (2 * i, 2 * i + 1)][-1])
            tve = st("v", [tvm], lambda v, i=i:
                     v.tensor_copy(b["vtok"][i][:], PB[2][:, 0:512]))
            self.pb_free[2] = tve
        t_vtok = self.tok(S["V"])

        def g_tp(sy):
            for p in range(NP):
                ins = sy.dma_start(
                    out=b["gtokT"][:].rearrange("p (i f) -> p i f", f=512)
                    [:, :, p * 128:(p + 1) * 128],
                    in_=b["gfeat"][wp][p][:, 0:W], transpose=True)
                self._inc(ins, S["tp"], 16)
        tgt0 = self.tok(S["tp"])
        _tpf = self.tok(S["tp"])
        self.blk.sync(lambda sy: (self._wait(sy, _tpf),
                                  self._wait(sy, t_qkv), g_tp(sy)))
        t_gt = self.tok(S["tp"])

        tcm = st("t", [t11, self.bank(3)], lambda t: [
            t.matmul(PB[3][:, 8 * i:8 * i + 8],
                     b["rb_coef"][wp][:, 128 * i:128 * (i + 1)],
                     cs["cI128"][0:8, 0:8], start=(i == 0), stop=(i == NTW - 1))
            for i in range(NTW)][-1])
        tcc = st("v", [tcm], lambda v:
                 v.tensor_copy(b["coefcol"][:, 0:NTW * 8], PB[3][:, 0:NTW * 8]))
        self.pb_free[3] = tcc
        yield

        t_wc = {}
        t_o = t_s = None
        pend = None

        def emit_om(ccp, gcp):
            par = (ccp % 2) * 64
            bk = 2 + (ccp % 2)
            t_omp = st("t", [t_wc[gcp % 2], self.bank(bk)],
                       lambda t, ccp=ccp, gcp=gcp, par=par, bk=bk: [
                (t.matmul(PB[bk][par:par + 64,
                                 p * 128 + u * 64: p * 128 + u * 64 + 64],
                          b["qt"][wp][p][64 * u:64 * u + 64,
                                         ccp * 64:(ccp + 1) * 64],
                          b["Ssb"][(gcp - 1) % 2][64 * u:64 * u + 64,
                                                  p * 64:(p + 1) * 64],
                          start=(p == 0), stop=False,
                          tile_position=(64 * u, par))
                 if part == 0 else
                 t.matmul(PB[bk][par:par + 64,
                                 p * 128 + u * 64: p * 128 + u * 64 + 64],
                          b["LT"][wp][p][64 * u:64 * u + 64,
                                         ccp * 64:(ccp + 1) * 64],
                          b["Wcb"][gcp % 2][64 * u:64 * u + 64,
                                            p * 64:(p + 1) * 64],
                          start=False, stop=(p == NP - 1),
                          tile_position=(64 * u, par)))
                for u in range(2) for part in (0, 1) for p in range(NP)][-1])
            t_op = st("v", [t_omp], lambda v, ccp=ccp, par=par, bk=bk:
                      v.tensor_copy(b["otok"][ccp // 2][par:par + 64, 0:512],
                                    PB[bk][par:par + 64, 0:512]))
            self.pb_free[bk] = t_op
            return t_op

        for cc in range(NCH):
            gc = self.gch
            waits = [pt for toks in pair_tok for pt in toks] if cc == 0 else []
            t_wcm = st("t", waits + [self.bank(0), t_s or (S["V"], 0)],
                       lambda t, cc=cc, gc=gc: [
                t.matmul(PB[0][64 * u:64 * u + 64, p * 64:(p + 1) * 64],
                         b["Y1T"][wp][p][64 * u:64 * u + 64,
                                         cc * 64:(cc + 1) * 64],
                         b["Ssb"][(gc - 1) % 2][64 * u:64 * u + 64,
                                                p * 64:(p + 1) * 64],
                         start=(p == 0), stop=(p == NP - 1),
                         tile_position=(64 * u, 64 * u))
                for u in range(2) for p in range(NP)][-1])
            t_wc[gc % 2] = st("v", [t_wcm], lambda v, cc=cc, gc=gc:
                      v.tensor_sub(
                          b["Wcb"][gc % 2][:, 0:NP * 64]
                          .rearrange("p (a q) -> p a q", a=NP),
                          PB[0][:, 0:NP * 64].rearrange("p (a q) -> p a q",
                                                        a=NP),
                          b["X"][wp][:, 0:].rearrange(
                              "p (a c q) -> p a c q", a=NP, c=NCH)
                          [:, :, cc, 64:128]))
            self.pb_free[0] = t_wc[gc % 2]

            if pend is not None:
                t_o = emit_om(*pend)

            t_sm = st("t", [t_wc[gc % 2], self.bank(1)], lambda t, cc=cc,
                      gc=gc: [
                t.matmul(PB[1][64 * u:64 * u + 64, p * 64:(p + 1) * 64],
                         b["KT"][wp][p][64 * u:64 * u + 64,
                                        cc * 64: cc * 64 + 64],
                         b["Wcb"][gc % 2][64 * u:64 * u + 64,
                                          p * 64:(p + 1) * 64],
                         start=(p == 0), stop=(p == NP - 1),
                         tile_position=(64 * u, 64 * u))
                for u in range(2) for p in range(NP)][-1])
            t_s = st("v", [t_sm], lambda v, cc=cc, gc=gc: (
                [v.scalar_tensor_tensor(
                    b["Sst"][:, p * 64:(p + 1) * 64],
                    b["Sst"][:, p * 64:(p + 1) * 64],
                    b["gCcol"][wp][:, cc * NP + p: cc * NP + p + 1],
                    PB[1][:, p * 64:(p + 1) * 64],
                    ALU.mult, ALU.subtract) for p in range(NP)],
                v.tensor_copy(b["Ssb"][gc % 2][:], b["Sst"][:]))[-1])
            self.pb_free[1] = t_s
            pend = (cc, gc)
            self.gch += 1
            yield

        t_o = emit_om(*pend)
        yield

        for i in range(NTW):
            tD = st("v", [t_o, t_vtok, tcc, self.tok(S["tp"])],
                    lambda v, i=i: (
                v.tensor_mul(b["on2"][:, 0:512].rearrange("p (h q) -> p h q", h=8),
                             b["vtok"][i][:].rearrange("p (h q) -> p h q", h=8),
                             b["coefcol"][:, 8 * i:8 * i + 8]
                             .rearrange("p (h o) -> p h o", o=1)
                             .broadcast_to([128, 8, 64])),
                v.tensor_add(b["otok"][i][:], b["otok"][i][:], b["on2"][:, 0:512]),
                v.tensor_mul(b["on2"][:, 0:512], b["otok"][i][:],
                             b["otok"][i][:]),
                v.tensor_reduce(
                    b["orst"][:], b["on2"][:, 0:512]
                    .rearrange("p (h q) -> p h q", h=8),
                    AX.X, ALU.add))[-1])
            tsr = st("a", [tD], lambda a:
                     a.activation(b["orst"][:], b["orst"][:], AF.Sqrt,
                                  bias=1e-6, scale=1.0 / 64))
            tn1 = st("v", [tsr], lambda v, i=i: (
                v.reciprocal(b["orst"][:], b["orst"][:]),
                v.tensor_mul(b["on2"][:, 0:512].rearrange("p (h q) -> p h q", h=8),
                             b["otok"][i][:].rearrange("p (h q) -> p h q", h=8),
                             b["orst"][:].rearrange("p (h o) -> p h o", o=1)
                             .broadcast_to([128, 8, 64])),
                v.tensor_mul(b["on2"][:, 0:512], b["on2"][:, 0:512],
                             b["onormT"][:]))[-1])
            tsi = st("a", [tn1], lambda a:
                     a.activation(b["osil"][:], b["on2"][:, 0:512], AF.Silu))
            to2 = st("v", [tsi, t_gt, getattr(self, "war_o2f", (S["tp"], 0))],
                     lambda v, i=i:
                     v.tensor_mul(b["o2f"][:, 0:512], b["osil"][:],
                                  b["gtokT"][:, i * 512:(i + 1) * 512]))
            self.step("s", [to2, getattr(self, "war_o2ff", (S["T"], 0)),
                            self.tok(S["tp"])],
                      lambda sy: (
                self._tp(sy, b["o2ff"][:].rearrange("p (c t) -> p c t", c=4),
                         b["o2f"][:, 0:512]), None)[-1], dma=True)
            ttp = self.tok(S["tp"])
            self.war_o2f = ttp

            for kdh in range(2):
                bk = kdh
                twm = st("t", [ttp, self.bank(bk), self.wldg], lambda t, kdh=kdh, bk=bk: [
                    t.matmul(PB[bk][:, 0:512],
                             b["o2ff"][:, hv * 128:(hv + 1) * 128],
                             b["wo_s"][:, hv * 1024 + kdh * 512:
                                       hv * 1024 + kdh * 512 + 512],
                             start=(hv == 0), stop=(hv == 3))
                    for hv in range(4)][-1])
                po = ((w % self.GW) * NTW + i) * 1024 + kdh * 512
                tev = st("v", [twm, getattr(self, "war_pout", (S["st"], 0))],
                         lambda v, po=po, bk=bk:
                         v.tensor_copy(b["pstage"][:, po:po + 512],
                                       PB[bk][:, 0:512]))
                self.pb_free[bk] = tev
            self.war_o2ff = self.tok(S["T"])
            t_po = tev

            if i == NTW - 1 and w % self.GW == self.GW - 1:
                GJ = self.GW * NTW
                _stf = self.tok(S["st"])
                def store(sy, w=w, t_po=t_po, _stf=_stf, GJ=GJ):
                    self._wait(sy, _stf)
                    self._wait(sy, t_po)
                    pv = self.psh.ap().rearrange(
                        "s (g j p) h -> g s p j h", j=GJ, p=128)
                    ins = sy.dma_start(
                        out=pv[w // self.GW][self.dynv["oslot"]],
                        in_=b["pstage"][:].rearrange("p (j h) -> p j h", j=GJ))
                    self._inc(ins, S["st"], 16)
                self.blk.sync(store)
                self.war_pout = self.tok(S["st"])
            yield

    def _x_phase(self):
        nc, S, PB, cs = self.nc, self.S, self.PB, self.cs
        TO, NTO = self.TO, self.NTO
        st = self.step
        xtok, xsq, statc = self.xtok, self.xsq, self.statc
        GX = min(4, NTO)
        xps = ExitStack()
        pshb2 = [xps.enter_context(
            nc.sbuf_tensor(f"pshb{j}", [128, 2 * GX * 1024], F32))
            for j in range(2)]
        self.rowsc = xps.enter_context(
            nc.sbuf_tensor("rowsc", [16, 1024], F32))
        with nc.Block() as blk:
            self.blk = blk

            def _gsnap(gp):
                pid = gp.partition_id()
                self.dynv["rb"] = gp.snap(pid % 2 * TO, min_val=0, max_val=TO)
            self.blk.gpsimd(_gsnap)

            xw = [xps.enter_context(nc.sbuf_tensor(f"xw{j}", [128, HID], F32))
                  for j in range(2)]
            xq2 = [xps.enter_context(nc.sbuf_tensor(f"xq2_{j}", [128, HID],
                                                    F32)) for j in range(2)]
            t_prev = {0: (S["A"], 0), 1: (S["A"], 0)}
            war_stx = {0: (S["st"], 0), 1: (S["st"], 0)}
            war_xh = {0: (S["T"], 0), 1: (S["T"], 0)}
            t_gld = (S["gld"], 0)
            for tt in range(NTO):
                par = tt % 2
                sc = par * 8
                if tt % GX == 0:
                    g = tt // GX
                    wpb = getattr(self, "war_pshb", {})
                    self.war_pshb = wpb
                    t_gld = self.dma("g", [wpb.get(g % 2, (S["V"], 0))],
                                     lambda gp, g=g: [
                        gp.dma_start(
                            out=pshb2[g % 2]
                            [:, s * GX * 1024:(s + 1) * GX * 1024]
                            .rearrange("p (i h) -> p i h", i=GX),
                            in_=self.psh.ap()[s]
                            [bass.ds(self.dynv["rb"], TO), :]
                            [128 * GX * g:128 * GX * (g + 1), :]
                            .rearrange("(i p) h -> p i h", p=128))
                        for s in range(2)])
                pshb = pshb2[(tt // GX) % 2]
                def ld(sy, tt=tt, par=par):
                    ins = sy.dma_start(
                        out=xtok[par][:],
                        in_=self.xh_in[128 * tt:128 * (tt + 1), :])
                    self._inc(ins, S["ld"], 16)
                self.blk.sync(lambda sy, tt=tt, par=par:
                              (self._wait(sy, t_prev[par]), ld(sy, tt, par)))
                t_ld = self.tok(S["ld"])

                ps0 = (tt % GX) * 1024
                ps1 = GX * 1024 + (tt % GX) * 1024
                tv = st("v", [t_gld, war_stx[par]],
                        lambda v, ps0=ps0, ps1=ps1, par=par, sc=sc: (
                    v.tensor_add(xw[par][:], pshb[:, ps0:ps0 + 1024],
                                 pshb[:, ps1:ps1 + 1024]),
                    v.tensor_mul(xq2[par][:], xw[par][:], xw[par][:]),
                    v.reduce_sum(statc[:, sc:sc + 1], xw[par][:], AX.X),
                    v.reduce_sum(statc[:, sc + 1:sc + 2], xq2[par][:], AX.X),
                    v.tensor_scalar_mul(statc[:, sc:sc + 1],
                                        statc[:, sc:sc + 1], 1.0 / HID),
                    v.tensor_scalar_mul(statc[:, sc + 1:sc + 2],
                                        statc[:, sc + 1:sc + 2], 1.0 / HID),
                    v.scalar_tensor_tensor(statc[:, sc + 2:sc + 3],
                                           statc[:, sc:sc + 1],
                                           statc[:, sc:sc + 1],
                                           statc[:, sc + 1:sc + 2],
                                           ALU.mult, ALU.subtract))[-1])
                if tt % GX == GX - 1:
                    self.war_pshb[(tt // GX) % 2] = tv
                ta = st("a", [tv], lambda a, sc=sc:
                        a.activation(statc[:, sc + 3:sc + 4],
                                     statc[:, sc + 2:sc + 3], AF.Sqrt,
                                     bias=1e-5, scale=-1.0))
                tf = st("v", [ta, t_ld, war_xh[par]],
                        lambda v, par=par, sc=sc: (
                    v.reciprocal(statc[:, sc + 3:sc + 4],
                                 statc[:, sc + 3:sc + 4]),
                    v.tensor_scalar(xw[par][:], xw[par][:],
                                    statc[:, sc:sc + 1],
                                    statc[:, sc + 3:sc + 4],
                                    ALU.subtract, ALU.mult),
                    v.tensor_add(xw[par][:], xw[par][:], xtok[par][:]),
                    v.tensor_copy(self.xhatt[par][:], xw[par][:]))[-1])
                t_prev[par] = tf

                _stf = self.tok(S["st"])
                def stx(sy, tt=tt, tf=tf, _stf=_stf, par=par):
                    self._wait(sy, _stf)
                    self._wait(sy, tf)
                    ins = sy.dma_start(out=self.x2buf.ap()
                                       [128 * tt:128 * (tt + 1), :],
                                       in_=xw[par][:])
                    self._inc(ins, S["st"], 16)
                self.blk.sync(stx)
                war_stx[par] = self.tok(S["st"])

                tcm = st("t", [tf, self.bank(2) if tt == 0 else (S["V"], 0)],
                         lambda t, tt=tt, par=par: (
                    t.matmul(PB[2][0:1, 0:512], cs["cONESC"][:],
                             self.xhatt[par][:, 0:512], start=(tt == 0),
                             stop=(tt == NTO - 1)),
                    t.matmul(PB[3][0:1, 0:512], cs["cONESC"][:],
                             self.xhatt[par][:, 512:1024], start=(tt == 0),
                             stop=(tt == NTO - 1)))[-1])
                war_xh[par] = tcm
            tse = st("v", [tcm], lambda v: (
                v.tensor_copy(self.rowsc[0:1, 0:512], PB[2][0:1, 0:512]),
                v.tensor_copy(self.rowsc[0:1, 512:1024], PB[3][0:1, 0:512]))[-1])
            self.pb_free[2] = tse
            self.pb_free[3] = tse

            _stf = self.tok(S["st"])
            def stsum(sy, tse=tse, _stf=_stf):
                self._wait(sy, _stf)
                self._wait(sy, tse)
                pid = sy.partition_id()
                slot = sy.snap(pid % 2, min_val=0, max_val=1)
                ins = sy.dma_start(out=self.shsum.ap()[slot],
                                   in_=self.rowsc[0:1, 0:1024])
                self._inc(ins, S["st"], 16)
            self.blk.sync(stsum)
            self.blk.gpsimd(lambda gp: gp.wait_ge(S["st"],
                                                  self.cnt.get(S["st"].name, 0)))
        xps.close()

    def _f_phase(self):
        nc, S, PB, cs = self.nc, self.S, self.PB, self.cs
        TO, NTO = self.TO, self.NTO
        NFW = TO // 512 if TO >= 512 else 1
        FW = min(512, TO)
        st = self.step
        ffs = ExitStack()
        sb = lambda n, s, d=BF16: ffs.enter_context(nc.sbuf_tensor(n, s, d))
        fw1 = sb("fw1_s", [128, 8 * 1024])
        fw2 = sb("fw2_s", [128, 8 * 512])
        fw3 = sb("fw3_s", [128, 4 * 256])
        fw4 = sb("fw4_s", [128, 2 * 1])
        gw1 = sb("gw1_s", [128, 8 * 512])
        gw2 = sb("gw2_s", [128, 4 * 256])
        gw3 = sb("gw3_s", [128, 2 * 1])
        frows = sb("frows", [1, 2048 + 64], F32)
        fbcol = sb("fbcol", [128, 14], F32)
        fb4v = sb("fb4v", [1, 8], F32)
        gb1r = sb("gb1r", [1, 512], F32)
        gb2r = sb("gb2r", [1, 256], F32)
        outA = sb("outA", [128, 8 * FW])
        outB = sb("outB", [128, 8 * FW])
        x2f = sb("x2f", [128, 8 * FW])
        h1 = sb("h1", [128, 8 * 512])
        h2 = sb("h2", [128, 4 * 512])
        h3 = sb("h3", [128, 2 * 512])
        tfr = sb("tfr", [1, 512], F32)
        cplr = sb("cplr", [1, 512], F32)
        cplrb = sb("cplrb", [1, 512])
        cplP = sb("cplP", [128, 512])
        cplN = sb("cplN", [128, 512])
        fsc = sb("fsc", [128, 512])
        grow = sb("grow", [1, 1024], F32)
        growb = sb("growb", [1, 1024])
        gcolT = sb("gcolT", [128, 16])
        lnr = sb("lnr", [1, 512], F32)
        lnm = sb("lnm", [1, 512], F32)
        lne = sb("lne", [1, 512], F32)
        lnu = sb("lnu", [1, 512], F32)
        lns = sb("lns", [1, 512], F32)
        lnrb = sb("lnrb", [2, 512])
        lnr2b = sb("lnr2b", [1, 512])
        fsq = sb("fsq", [128, 512])
        xo3 = sb("xo3", [128, 512], F32)

        with nc.Block() as blk:
            self.blk = blk
            t_w = self.dma("g", [], lambda gp: [
                gp.dma_start(out=fw1[:].rearrange("p (a b) -> p a b", a=8),
                             in_=self.ffn_in["fw1"].rearrange("a p b -> p a b")),
                gp.dma_start(out=fw2[:].rearrange("p (a b) -> p a b", a=8),
                             in_=self.ffn_in["fw2"].rearrange("a p b -> p a b")),
                gp.dma_start(out=fw3[:].rearrange("p (a b) -> p a b", a=4),
                             in_=self.ffn_in["fw3"].rearrange("a p b -> p a b")),
                gp.dma_start(out=fw4[:].rearrange("p (a b) -> p a b", a=2),
                             in_=self.ffn_in["fw4"].rearrange("a p b -> p a b"),
                             allow_slow_non_contiguous=True),
                gp.dma_start(out=gw1[:].rearrange("p (a b) -> p a b", a=8),
                             in_=self.ffn_in["gw1"].rearrange("a p b -> p a b")),
                gp.dma_start(out=gw2[:].rearrange("p (a b) -> p a b", a=4),
                             in_=self.ffn_in["gw2"].rearrange("a p b -> p a b")),
                gp.dma_start(out=gw3[:].rearrange("p (a b) -> p a b", a=2),
                             in_=self.ffn_in["gw3"].rearrange("a p b -> p a b"),
                             allow_slow_non_contiguous=True),
                gp.dma_start(out=frows[0:1, 0:1024], in_=self.ffn_in["fb1"]),
                gp.dma_start(out=frows[0:1, 1024:1536], in_=self.ffn_in["fb2"]),
                gp.dma_start(out=frows[0:1, 1536:1792], in_=self.ffn_in["fb3"]),
                gp.dma_start(out=fb4v[0:1, 0:1], in_=self.ffn_in["fb4"]),
                gp.dma_start(out=fb4v[0:1, 1:2], in_=self.ffn_in["gb3"]),
                gp.dma_start(out=fb4v[0:1, 2:3], in_=self.ffn_in["arange"]),
                gp.dma_start(out=gb1r[0:1, :], in_=self.ffn_in["gb1"]),
                gp.dma_start(out=gb2r[0:1, :], in_=self.ffn_in["gb2"]),
            ])

            prow = ffs.enter_context(nc.sbuf_tensor("prow", [1, 1024], F32))
            prow2 = ffs.enter_context(nc.sbuf_tensor("prow2", [1, 1024], F32))
            t_ps = self.dma("s", [], lambda sy: [
                sy.dma_start(out=prow[0:1, :], in_=self.shsum.ap()[0]),
                sy.dma_start(out=prow2[0:1, :], in_=self.shsum.ap()[1])])

            tg1 = st("v", [t_ps, t_w], lambda v: (
                v.tensor_add(grow[0:1, :], prow2[0:1, :],
                             prow[0:1, :]),
                v.tensor_scalar_mul(grow[0:1, :], grow[0:1, :], 1.0 / self.T),
                v.tensor_copy(growb[0:1, :], grow[0:1, :]))[-1])
            tg2 = st("t", [tg1, self.bank(4)], lambda t: [
                t.matmul(PB[4][:, j:j + 1], growb[0:1, 128 * j:128 * (j + 1)],
                         cs["cONES8"][0:1, 0:1], start=(j == 0), stop=(j == 7))
                for j in range(8)][-1])
            tg3 = st("v", [tg2], lambda v:
                     v.tensor_copy(gcolT[:, 0:8], PB[4][:, 0:8]))
            self.pb_free[4] = tg3
            tg4 = st("t", [tg3, self.bank(5)], lambda t: [
                t.matmul(PB[5][0:1, 0:512], gcolT[:, dc:dc + 1],
                         gw1[:, dc * 512:(dc + 1) * 512],
                         start=(dc == 0), stop=(dc == 7)) for dc in range(8)][-1])
            tg5 = st("v", [tg4], lambda v:
                     v.tensor_add(lnr[0:1, 0:512], PB[5][0:1, 0:512], gb1r[0:1, :]))
            self.pb_free[5] = tg5
            tg6 = st("a", [tg5], lambda a:
                     a.activation(lnr[0:1, 0:512], lnr[0:1, 0:512], AF.Gelu))
            tg7 = st("v", [tg6], lambda v:
                     v.tensor_copy(lnrb[0:1, 0:512], lnr[0:1, 0:512]))
            tg8 = st("t", [tg7, self.bank(4)], lambda t: [
                t.matmul(PB[4][:, 8 + j:9 + j], lnrb[0:1, 128 * j:128 * (j + 1)],
                         cs["cONES8"][0:1, 0:1], start=(j == 0), stop=(j == 3))
                for j in range(4)][-1])
            tg9 = st("v", [tg8], lambda v:
                     v.tensor_copy(gcolT[:, 8:12], PB[4][:, 8:12]))
            self.pb_free[4] = tg9
            tg10 = st("t", [tg9, self.bank(5)], lambda t: [
                t.matmul(PB[5][0:1, 0:256], gcolT[:, 8 + dc:9 + dc],
                         gw2[:, dc * 256:(dc + 1) * 256],
                         start=(dc == 0), stop=(dc == 3)) for dc in range(4)][-1])
            tg11 = st("v", [tg10], lambda v:
                      v.tensor_add(lnr[0:1, 0:256], PB[5][0:1, 0:256],
                                   gb2r[0:1, :]))
            self.pb_free[5] = tg11
            tg12 = st("a", [tg11], lambda a:
                      a.activation(lnr[0:1, 0:256], lnr[0:1, 0:256], AF.Gelu))
            tg13 = st("v", [tg12], lambda v:
                      v.tensor_copy(lnrb[0:1, 0:256], lnr[0:1, 0:256]))
            tg14 = st("t", [tg13, self.bank(4)], lambda t: [
                t.matmul(PB[4][:, 12 + j:13 + j], lnrb[0:1, 128 * j:128 * (j + 1)],
                         cs["cONES8"][0:1, 0:1], start=(j == 0), stop=(j == 1))
                for j in range(2)][-1])
            tg15 = st("v", [tg14], lambda v:
                      v.tensor_copy(gcolT[:, 12:14], PB[4][:, 12:14]))
            self.pb_free[4] = tg15
            tg16 = st("t", [tg15, self.bank(5)], lambda t: [
                t.matmul(PB[5][0:1, 0:1], gcolT[:, 12 + dc:13 + dc],
                         gw3[:, dc:dc + 1], start=(dc == 0), stop=(dc == 1))
                for dc in range(2)][-1])
            tg17 = st("v", [tg16], lambda v:
                      v.tensor_scalar_add(fb4v[0:1, 3:4], PB[5][0:1, 0:1],
                                          fb4v[0:1, 1:2]))
            self.pb_free[5] = tg17
            tg18 = st("a", [tg17], lambda a:
                      a.activation(fb4v[0:1, 3:4], fb4v[0:1, 3:4], AF.Sigmoid))
            t_gf = st("v", [tg18], lambda v: (
                v.tensor_scalar_mul(fb4v[0:1, 4:5], fb4v[0:1, 3:4], 0.7),
                v.tensor_scalar_mul(fb4v[0:1, 5:6], fb4v[0:1, 2:3], 2.0))[-1])

            tb1 = st("v", [t_w], lambda v: (
                v.tensor_copy(lnrb[0:1, 0:512], frows[0:1, 0:512]),
                v.tensor_copy(lnr2b[0:1, 0:512], frows[0:1, 512:1024]))[-1])
            tb2 = st("t", [tb1, self.bank(4)], lambda t: (
                [t.matmul(PB[4][:, j:j + 1], lnrb[0:1, 128 * j:128 * (j + 1)],
                          cs["cONES8"][0:1, 0:1], start=(j == 0), stop=False)
                 for j in range(4)],
                [t.matmul(PB[4][:, 4 + j:5 + j], lnr2b[0:1, 128 * j:128 * (j + 1)],
                          cs["cONES8"][0:1, 0:1], start=False, stop=False)
                 for j in range(4)])[-1][-1])
            tb3 = st("v", [tb2], lambda v:
                     v.tensor_copy(lnrb[0:1, 0:512], frows[0:1, 1024:1536]))
            tb4 = st("t", [tb3], lambda t: (
                [t.matmul(PB[4][:, 8 + j:9 + j], lnrb[0:1, 128 * j:128 * (j + 1)],
                          cs["cONES8"][0:1, 0:1], start=False, stop=False)
                 for j in range(4)])[-1])
            tb5 = st("v", [tb4], lambda v:
                     v.tensor_copy(lnrb[0:1, 0:256], frows[0:1, 1536:1792]))
            tb6 = st("t", [tb5], lambda t: (
                [t.matmul(PB[4][:, 12 + j:13 + j],
                          lnrb[0:1, 128 * j:128 * (j + 1)],
                          cs["cONES8"][0:1, 0:1], start=False, stop=(j == 1))
                 for j in range(2)])[-1])
            tb7 = st("v", [tb6], lambda v:
                     v.tensor_copy(fbcol[:, 0:14], PB[4][:, 0:14]))
            self.pb_free[4] = tb7

            statc = self.statc
            fxq = [ffs.enter_context(nc.sbuf_tensor(f"fxq{j}", [128, HID],
                                                    F32)) for j in range(2)]
            xhR = ffs.enter_context(nc.sbuf_tensor("xhR", [128, HID], BF16))
            t_prevF = {0: [(S["A"], 0)], 1: [(S["A"], 0)]}
            war_tpA = {0: (S["tp"], 0), 1: (S["tp"], 0)}
            NTF = FW // 128
            for fw in range(NFW):
                fo = fw * FW
                for tl in range(NTF):
                    tt = fw * NTF + tl
                    par = tt % 2
                    sc = par * 8
                    t_ld = self.dma("s", list(t_prevF[par]),
                                    lambda sy, tt=tt, par=par: [
                        sy.dma_start(out=self.xtok[par][:],
                                     in_=self.x2buf.ap()
                                     [128 * tt:128 * (tt + 1), :])])
                    tv = st("v", [t_ld], lambda v, par=par, sc=sc: (
                        v.tensor_mul(fxq[par][:], self.xtok[par][:],
                                     self.xtok[par][:]),
                        v.reduce_sum(statc[:, sc:sc + 1], self.xtok[par][:],
                                     AX.X),
                        v.reduce_sum(statc[:, sc + 1:sc + 2], fxq[par][:],
                                     AX.X),
                        v.tensor_scalar_mul(statc[:, sc:sc + 1],
                                            statc[:, sc:sc + 1], 1.0 / HID),
                        v.tensor_scalar_mul(statc[:, sc + 1:sc + 2],
                                            statc[:, sc + 1:sc + 2],
                                            1.0 / HID),
                        v.scalar_tensor_tensor(statc[:, sc + 2:sc + 3],
                                               statc[:, sc:sc + 1],
                                               statc[:, sc:sc + 1],
                                               statc[:, sc + 1:sc + 2],
                                               ALU.mult, ALU.subtract))[-1])
                    ta = st("a", [tv], lambda a, sc=sc:
                            a.activation(statc[:, sc + 3:sc + 4],
                                         statc[:, sc + 2:sc + 3], AF.Sqrt,
                                         bias=1e-5, scale=-1.0))
                    tf = st("v", [ta, war_tpA[par]],
                            lambda v, par=par, sc=sc: (
                            v.reciprocal(statc[:, sc + 3:sc + 4],
                                         statc[:, sc + 3:sc + 4]),
                            v.tensor_scalar(self.xhatt[par][:],
                                            self.xtok[par][:],
                                            statc[:, sc:sc + 1],
                                            statc[:, sc + 3:sc + 4],
                                            ALU.subtract, ALU.mult))[-1])
                    tcp = st("v", [t_ld, self.tok(S["tp"])],
                             lambda v, par=par:
                             v.tensor_copy(xhR[:], self.xtok[par][:]))
                    t_prevF[par] = [tf, tcp]

                    _tpf2 = self.tok(S["tp"])
                    war_f = getattr(self, "war_fentry", [])
                    def tps(sy, tl=tl, tf=tf, tcp=tcp, _tpf2=_tpf2,
                            war_f=war_f, par=par):
                        self._wait(sy, _tpf2)
                        self._wait(sy, tf)
                        self._wait(sy, tcp)
                        for tk in war_f:
                            self._wait(sy, tk)
                        self._tp(sy, outA[:].rearrange("p (c q) -> p c q", c=8)
                                 [:, :, 128 * tl:128 * (tl + 1)],
                                 self.xhatt[par][:])
                        self._tp(sy, x2f[:].rearrange("p (c q) -> p c q", c=8)
                                 [:, :, 128 * tl:128 * (tl + 1)],
                                 xhR[:])
                    self.blk.sync(tps)
                    war_tpA[par] = self.tok(S["tp"])
                t_entry = self.tok(S["tp"])

                IN, OUT = outA, outB
                t_tw = t_entry
                for cyc in range(3):
                    for oc in range(8):
                        bk = oc % 4
                        tm = st("t", [self.bank(bk)] +
                                ([t_tw, t_w, tb7, t_gf] if oc == 0 else []),
                                lambda t, oc=oc, bk=bk, fo=fo: [
                            t.matmul(PB[bk][:, 0:FW],
                                     fw1[:, dc * 1024 + oc * 128:
                                         dc * 1024 + oc * 128 + 128],
                                     IN[:, dc * FW:(dc + 1) * FW],
                                     start=(dc == 0), stop=(dc == 7))
                            for dc in range(8)][-1])
                        te = st("a", [tm], lambda a, oc=oc, bk=bk:
                                a.activation(h1[:, oc * 512:oc * 512 + FW],
                                             PB[bk][:, 0:FW], AF.Gelu,
                                             bias=fbcol[:, oc:oc + 1]))
                        self.pb_free[bk] = te
                    t_h1 = self.tok(S["A"])
                    for oc in range(4):
                        bk = oc % 4
                        tm = st("t", [self.bank(bk)] + ([t_h1] if oc == 0 else []),
                                lambda t, oc=oc, bk=bk: [
                            t.matmul(PB[bk][:, 0:FW],
                                     fw2[:, dc * 512 + oc * 128:
                                         dc * 512 + oc * 128 + 128],
                                     h1[:, dc * 512:dc * 512 + FW],
                                     start=(dc == 0), stop=(dc == 7))
                            for dc in range(8)][-1])
                        te = st("a", [tm], lambda a, oc=oc, bk=bk:
                                a.activation(h2[:, oc * 512:oc * 512 + FW],
                                             PB[bk][:, 0:FW], AF.Gelu,
                                             bias=fbcol[:, 4 + oc:5 + oc]))
                        self.pb_free[bk] = te
                    t_h2 = self.tok(S["A"])
                    for oc in range(2):
                        bk = oc % 2
                        tm = st("t", [self.bank(bk)] + ([t_h2] if oc == 0 else []),
                                lambda t, oc=oc, bk=bk: [
                            t.matmul(PB[bk][:, 0:FW],
                                     fw3[:, dc * 256 + oc * 128:
                                         dc * 256 + oc * 128 + 128],
                                     h2[:, dc * 512:dc * 512 + FW],
                                     start=(dc == 0), stop=(dc == 3))
                            for dc in range(4)][-1])
                        te = st("a", [tm], lambda a, oc=oc, bk=bk:
                                a.activation(h3[:, oc * 512:oc * 512 + FW],
                                             PB[bk][:, 0:FW], AF.Gelu,
                                             bias=fbcol[:, 12 + oc:13 + oc]))
                        self.pb_free[bk] = te
                    t_h3 = self.tok(S["A"])
                    tm4 = st("t", [te, t_h3, self.bank(6)], lambda t: [
                        t.matmul(PB[6][0:1, 0:FW], fw4[:, dc:dc + 1],
                                 h3[:, dc * 512:dc * 512 + FW],
                                 start=(dc == 0), stop=(dc == 1))
                        for dc in range(2)][-1])
                    tt4 = st("v", [tm4, t_gf], lambda v:
                             v.tensor_scalar_add(tfr[0:1, 0:FW], PB[6][0:1, 0:FW],
                                                 fb4v[0:1, 0:1]))
                    self.pb_free[6] = tt4
                    tt5 = st("a", [tt4], lambda a:
                             a.activation(tfr[0:1, 0:FW], tfr[0:1, 0:FW],
                                          AF.Sigmoid))
                    tt6 = st("v", [tt5], lambda v: (
                        v.tensor_scalar(cplr[0:1, 0:FW], tfr[0:1, 0:FW], 0.3,
                                        fb4v[0:1, 4:5], ALU.mult, ALU.add),
                        v.tensor_scalar(cplr[0:1, 0:FW], cplr[0:1, 0:FW], -0.5,
                                        fb4v[0:1, 5:6], ALU.add, ALU.mult),
                        v.tensor_scalar_add(cplr[0:1, 0:FW], cplr[0:1, 0:FW], 0.1),
                        v.tensor_copy(cplrb[0:1, 0:FW], cplr[0:1, 0:FW]))[-1])
                    tt7 = st("t", [tt6, self.bank(7)], lambda t:
                             t.matmul(PB[7][:, 0:FW], cs["cONESR"][:],
                                      cplrb[0:1, 0:FW], start=True, stop=True))
                    tt8 = st("v", [tt7], lambda v: (
                        v.tensor_copy(cplP[:, 0:FW], PB[7][:, 0:FW]),
                        v.tensor_scalar_mul(cplN[:, 0:FW], PB[7][:, 0:FW], -1.0))[-1])
                    self.pb_free[7] = tt8
                    sgn = {0: 1, 1: 1, 2: -1, 3: -1, 4: -1, 5: -1, 6: 1, 7: 1}
                    t_tw = st("v", [tt8], lambda v: [
                        (v.tensor_mul(fsc[:, 0:FW],
                                      IN[:, ((c + 4) % 8) * FW:
                                         ((c + 4) % 8 + 1) * FW],
                                      (cplP if sgn[c] > 0 else cplN)[:, 0:FW]),
                         v.tensor_add(OUT[:, c * FW:(c + 1) * FW],
                                      IN[:, c * FW:(c + 1) * FW],
                                      fsc[:, 0:FW]))[-1]
                        for c in range(8)][-1])
                    IN, OUT = OUT, IN
                FIN = IN

                tm = st("t", [t_tw, self.bank(2), self.bank(3)], lambda t: [
                    t.matmul(PB[2][0:1, 0:FW], cs["cONESC"][:],
                             FIN[:, c * FW:(c + 1) * FW],
                             start=(c == 0), stop=(c == 7))
                    for c in range(8)][-1])
                tsq_l = []
                for c in range(8):
                    ta = st("a", [tm if c == 0 else tsq_l[-1]],
                            lambda a, c=c:
                            a.activation(fsq[:, 0:FW],
                                         FIN[:, c * FW:(c + 1) * FW],
                                         AF.Square))
                    tb = st("t", [ta], lambda t, c=c:
                            t.matmul(PB[3][0:1, 0:FW], cs["cONESC"][:],
                                     fsq[:, 0:FW], start=(c == 0), stop=(c == 7)))
                    tsq_l.append(tb)
                trow = st("v", [tsq_l[-1]], lambda v: (
                    v.tensor_scalar_mul(lnm[0:1, 0:FW], PB[2][0:1, 0:FW],
                                        1.0 / HID),
                    v.tensor_scalar_mul(lne[0:1, 0:FW], PB[3][0:1, 0:FW],
                                        1.0 / HID),
                    v.tensor_tensor(lnu[0:1, 0:FW], lnm[0:1, 0:FW],
                                    lnm[0:1, 0:FW], ALU.mult),
                    v.tensor_sub(lnu[0:1, 0:FW], lnu[0:1, 0:FW],
                                 lne[0:1, 0:FW]))[-1])
                self.pb_free[2] = trow
                self.pb_free[3] = trow
                tsr2 = st("a", [trow], lambda a:
                          a.activation(lns[0:1, 0:FW], lnu[0:1, 0:FW], AF.Sqrt,
                                       bias=1e-5, scale=-1.0))
                tf2 = st("v", [tsr2], lambda v: (
                    v.reciprocal(lne[0:1, 0:FW], lns[0:1, 0:FW]),
                    v.tensor_tensor(lnu[0:1, 0:FW], lnm[0:1, 0:FW],
                                    lne[0:1, 0:FW], ALU.mult),
                    v.tensor_scalar_mul(lnu[0:1, 0:FW], lnu[0:1, 0:FW], -1.0),
                    v.tensor_copy(lnrb[0:1, 0:FW], lne[0:1, 0:FW]),
                    v.tensor_copy(lnr2b[0:1, 0:FW], lnu[0:1, 0:FW]))[-1])
                tbc1 = st("t", [tf2, self.bank(4), self.bank(5)], lambda t: (
                    t.matmul(PB[4][:, 0:FW], cs["cONESR"][:], lnrb[0:1, 0:FW],
                             start=True, stop=True),
                    t.matmul(PB[5][:, 0:FW], cs["cONESR"][:], lnr2b[0:1, 0:FW],
                             start=True, stop=True))[-1])
                last_store = getattr(self, "war_xo3", (S["st"], 0))
                for c in range(8):
                    tx3 = st("v", [tbc1, last_store], lambda v, c=c: (
                        v.tensor_mul(fsc[:, 0:FW],
                                     FIN[:, c * FW:(c + 1) * FW],
                                     PB[4][:, 0:FW]),
                        v.tensor_add(fsc[:, 0:FW], fsc[:, 0:FW], PB[5][:, 0:FW]),
                        v.tensor_add(xo3[:, 0:FW], fsc[:, 0:FW],
                                     x2f[:, c * FW:(c + 1) * FW]))[-1])
                    _stf = self.tok(S["st"])
                    def sout(sy, c=c, fo=fo, tx3=tx3, _stf=_stf):
                        self._wait(sy, _stf)
                        self._wait(sy, tx3)
                        ins = sy.dma_start(
                            out=self.out_t[128 * c:128 * (c + 1), fo:fo + FW],
                            in_=xo3[:, 0:FW])
                        self._inc(ins, S["st"], 16)
                    self.blk.sync(sout)
                    last_store = self.tok(S["st"])
                    self.war_xo3 = last_store
                self.pb_free[4] = tx3
                self.pb_free[5] = tx3
                self.war_fentry = [t_tw, tx3]
            self.blk.gpsimd(lambda gp: gp.wait_ge(S["st"],
                                                  self.cnt.get(S["st"].name, 0)))
        ffs.close()


def build_program(T=4096, W=256, debug=()):
    return Builder(T, W, debug).build()


def kernel(**inputs):
    x = np.asarray(inputs["L0_attn_x"])
    Bn, T, _ = x.shape
    nc = build_program(T=T, W=min(256, T))
    CSA, CSB = rope_tables(T)
    packs = [pack_attn(inputs, hh) for hh in (0, 1)]
    ffn = pack_ffn(inputs)
    in_maps = []
    for c in range(8):
        b, hh = c // 2, c % 2
        m = dict(x_in=np.ascontiguousarray(x[b]).astype(np.float32),
                 xh_in=np.ascontiguousarray(
                     x[b][hh * (T // 2):(hh + 1) * (T // 2)]).astype(np.float32),
                 csa=CSA, csb=CSB)
        m.update(packs[hh])
        m.update(ffn)
        in_maps.append(m)
    res = run_bass_kernel_spmd(nc, in_maps, core_ids=list(range(8)))
    out = np.empty((Bn, T, HID), np.float32)
    for c in range(8):
        b, hh = c // 2, c % 2
        out[b, hh * (T // 2):(hh + 1) * (T // 2), :] = res.results[c]["out"].T
    return out


# version 19
# speedup vs baseline: 1.0029x; 1.0029x over previous
import numpy as np
import ml_dtypes
from contextlib import ExitStack

import concourse.bass as bass
import concourse.mybir as mybir
from concourse.bass_utils import run_bass_kernel_spmd

F32 = mybir.dt.float32
BF16 = mybir.dt.bfloat16
AF = mybir.ActivationFunctionType
ALU = mybir.AluOpType
AX = mybir.AxisListType
BF = ml_dtypes.bfloat16

HID = 1024
NP = 4
C = 64
QK_ROWS = {(0, 0): 0, (1, 0): 32, (0, 1): 64, (1, 1): 96}


def _swap32():
    sw = np.zeros((128, 128), np.float32)
    for m in range(128):
        base = (m // 64) * 64
        sw[m, base + (m % 64 + 32) % 64] = 1
    return sw


def _perm_idx(hbase):
    ev, od = np.arange(0, 64, 2), np.arange(1, 64, 2)
    qk, v = [], []
    for hp in range(NP):
        h0, h1 = hbase + 2 * hp, hbase + 2 * hp + 1
        qk.append(np.concatenate([64 * h0 + ev, 64 * h0 + od,
                                  64 * h1 + ev, 64 * h1 + od]))
        v.append(np.concatenate([64 * h0 + np.arange(64), 64 * h1 + np.arange(64)]))
    return qk, v


def pack_attn(inputs, hh):
    g = lambda n: np.asarray(inputs[n], dtype=np.float32)
    hbase = hh * 8
    qk_idx, v_idx = _perm_idx(hbase)
    rows = []
    for hp in range(NP):
        rows.append(g("L0_attn_Wq")[qk_idx[hp]])
    for hp in range(NP):
        rows.append(g("L0_attn_Wk")[qk_idx[hp]])
    for hp in range(NP):
        rows.append(g("L0_attn_Wv")[v_idx[hp]])
    for hp in range(NP):
        rows.append(g("L0_attn_Wg")[v_idx[hp]])
    rows.append(g("L0_attn_Wgk")[hbase:hbase + 8])
    rows.append(g("L0_attn_Wb")[hbase:hbase + 8])
    WA = np.concatenate(rows, 0)
    wa = np.ascontiguousarray(WA.T).reshape(HID // 128, 128, 2064)
    cw = np.zeros((128, 48), np.float32)
    cb = np.zeros((128, 12), np.float32)
    cws = [g("L0_attn_cqw"), g("L0_attn_ckw"), g("L0_attn_cvw")]
    cbs = [g("L0_attn_cqb"), g("L0_attn_ckb"), g("L0_attn_cvb")]
    for qi in range(3):
        for hp in range(NP):
            idx = qk_idx[hp] if qi < 2 else v_idx[hp]
            for tap in range(4):
                cw[:, qi * 16 + hp * 4 + tap] = cws[qi][idx, 0, tap]
            cb[:, qi * 4 + hp] = cbs[qi][idx]
    woT = np.ascontiguousarray(g("L0_attn_Wo").T[64 * hbase: 64 * hbase + 512, :]
                               ).reshape(4, 128, 1024)
    sl = lambda n: np.ascontiguousarray(g(n)[hbase:hbase + 8].reshape(8, 1))
    return dict(wa=wa, cw=cw, cb=cb, wo=woT,
                bgk8=sl("L0_attn_bgk"), dtb8=sl("L0_attn_dt_bias"),
                bb8=sl("L0_attn_bb"), alog8=sl("L0_attn_A_log"),
                dp8=sl("L0_attn_Dp"),
                onorm=np.ascontiguousarray(g("L0_attn_onorm_w").reshape(1, 64)))


def rope_tables(T):
    i = np.arange(32)
    inv_freq = 10000.0 ** (-i / 32.0)
    jj = (2 * np.arange(32)) % 32
    t = np.arange(T, dtype=np.float64)
    ang = t[None, :] * inv_freq[jj][:, None]
    c = np.cos(ang); s = np.sin(ang)
    cdup = np.concatenate([c, c], 0)
    CSA = np.concatenate([cdup, cdup], 0).astype(ml_dtypes.bfloat16)
    CSB = np.concatenate([-s, s, -s, s], 0).astype(ml_dtypes.bfloat16)
    return np.ascontiguousarray(CSA), np.ascontiguousarray(CSB)


def pack_ffn(inputs):
    g = lambda n: np.asarray(inputs[n], dtype=np.float32)
    ch = lambda w: np.ascontiguousarray(w).reshape(w.shape[0] // 128, 128, w.shape[1])
    return dict(
        fw1=ch(g("L0_ffn_cn_w1")), fb1=g("L0_ffn_cn_b1").reshape(1, 1024),
        fw2=ch(g("L0_ffn_cn_w2")), fb2=g("L0_ffn_cn_b2").reshape(1, 512),
        fw3=ch(g("L0_ffn_cn_w3")), fb3=g("L0_ffn_cn_b3").reshape(1, 256),
        fw4=ch(g("L0_ffn_cn_w4")), fb4=g("L0_ffn_cn_b4").reshape(1, 1),
        gw1=ch(g("L0_ffn_gc_w1")), gb1=g("L0_ffn_gc_b1").reshape(1, 512),
        gw2=ch(g("L0_ffn_gc_w2")), gb2=g("L0_ffn_gc_b2").reshape(1, 256),
        gw3=ch(g("L0_ffn_gc_w3")), gb3=g("L0_ffn_gc_b3").reshape(1, 1),
        arange=g("L0_ffn_adaptive_range").reshape(1, 1))


class Builder:
    def __init__(self, T, W, debug=()):
        assert W % 128 == 0 and T % W == 0 and W % C == 0 and 128 <= W <= 512
        self.T, self.W = T, W
        self.NW = T // W
        self.NCH = W // C
        self.NTW = W // 128
        self.TO = T // 2
        self.NTO = self.TO // 128
        self.debug = set(debug)
        self.nc = bass.Bass("TRN2", target_bir_lowering=False, num_devices=8)
        self.cnt = {}
        self.pb_free = {}
        self._pe_last = None
        self._pe_cols = {0: None, 1: None}
        self._pe_drains = 0
        self.es = ExitStack()
        self.blk = None

    def tok(self, s):
        return (s, self.cnt.get(s.name, 0))

    def _inc(self, ins, s, n=1):
        ins.then_inc(s, n)
        self.cnt[s.name] = self.cnt.get(s.name, 0) + n

    @staticmethod
    def _wait(eng, token):
        s, v = token
        if v > 0:
            eng.wait_ge(s, v)

    _DRAIN_OPS = frozenset((
        "tensor_copy", "tensor_tensor", "tensor_add", "tensor_sub",
        "tensor_mul", "tensor_max", "tensor_scalar", "tensor_single_scalar",
        "tensor_scalar_add", "tensor_scalar_sub", "tensor_scalar_mul",
        "tensor_scalar_max", "tensor_scalar_min", "tensor_relu",
        "tensor_reduce", "reduce_max", "reduce_sum", "scalar_tensor_tensor",
        "tensor_tensor_scan", "memset", "memzero", "reciprocal", "activation",
        "copy", "mul", "add", "square", "sqrt", "bn_stats", "bn_aggr",
        "select", "transpose", "iota"))

    class _Drainer:

        def __init__(self, eng):
            self._e = eng
            self._n = 0
            self._outs = set()
            self._ins = set()
            self.last = None

        @staticmethod
        def _tnames(args, kwargs):
            got = []
            for x in list(args) + list(kwargs.values()):
                t = getattr(x, "tensor", None)
                if t is not None:
                    got.append(t.name)
            return got

        def __getattr__(self, name):
            fn = getattr(self._e, name)
            if name in Builder._DRAIN_OPS:
                def wrapped(*a, **k):
                    names = self._tnames(a, k)
                    out = set(names[:1])
                    ins = set(names[1:])
                    if (out & (self._outs | self._ins)) or (ins & self._outs):
                        self._e.drain()
                        self._outs.clear()
                        self._ins.clear()
                    self._outs |= out
                    self._ins |= ins
                    self._n += 1
                    self.last = fn(*a, **k)
                    return self.last
                return wrapped
            return fn

    class _PEGuard:

        def __init__(self, eng, owner):
            self._e = eng
            self._o = owner

        def matmul(self, out, lhsT, rhs, *a, **k):
            K = lhsT.shape[0]
            M = lhsT.shape[1]
            tp = k.get("tile_position", (0, 0))
            row = "F" if K > 64 else tp[0] // 64
            cols = (0, 1) if (K > 64 or M > 64) else (tp[1] // 64,)
            st = self._o._pe_cols
            if any(st[c] is not None and st[c] != row for c in cols):
                self._e.drain()
                self._o._pe_drains += 1
                st[0] = st[1] = None
            for c in cols:
                st[c] = row
            return self._e.matmul(out, lhsT, rhs, *a, **k)

        def __getattr__(self, name):
            return getattr(self._e, name)

    def step(self, eng, waits, fn, dma=False):
        S = self.S
        sem = {"t": S["T"], "v": S["V"], "a": S["A"], "g": S["G"],
               "s": S["SY"]}[eng]
        attr = {"t": "tensor", "v": "vector", "a": "scalar", "g": "gpsimd",
                "s": "sync"}[eng]
        me = self

        def body(e):
            for tkn in waits:
                me._wait(e, tkn)
            if eng in ("v", "a", "g"):
                ep = Builder._Drainer(e)
            elif eng == "t":
                ep = Builder._PEGuard(e, me)
            else:
                ep = e
            last = fn(ep)
            if dma:
                assert last is None
            else:
                if eng in ("v", "a", "g"):
                    if ep._n == 1 and ep.last is not None:
                        last = ep.last
                    else:
                        last = e.drain()
                elif last is None:
                    last = e.nop()
                me._inc(last, sem, 1)
        getattr(self.blk, attr)(body)
        return self.tok(sem)

    def dma(self, eng, waits, fn):
        S = self.S
        sem = S["gld"] if eng == "g" else S["ld"]
        me = self
        fence = self.tok(sem)

        def body(e):
            me._wait(e, fence)
            for tkn in waits:
                me._wait(e, tkn)
            for ins in fn(e):
                me._inc(ins, sem, 16)
        getattr(self.blk, {"s": "sync", "g": "gpsimd", "a": "scalar"}[eng])(body)
        return self.tok(sem)

    def bank(self, i):
        return self.pb_free.get(i, (self.S["V"], 0))

    def build(self):
        nc = self.nc
        T, W, NCH = self.T, self.W, self.NCH
        TO = self.TO
        es = self.es

        din = lambda n, s: nc.dram_tensor(n, s, F32, kind="ExternalInput").ap()
        self.x_in = din("x_in", [T, HID])
        self.xh_in = din("xh_in", [TO, HID])
        self.wa_in = din("wa", [HID // 128, 128, 2064])
        self.wo_in = din("wo", [4, 128, 1024])
        self.cw_in = din("cw", [128, 48])
        self.cb_in = din("cb", [128, 12])
        self.csa_in = nc.dram_tensor("csa", [128, T], BF16,
                                     kind="ExternalInput").ap()
        self.csb_in = nc.dram_tensor("csb", [128, T], BF16,
                                     kind="ExternalInput").ap()
        self.small_in = {n: din(n, [8, 1])
                         for n in ("bgk8", "dtb8", "bb8", "alog8", "dp8")}
        self.onorm_in = din("onorm", [1, 64])
        self.ffn_in = {n: din(n, s) for n, s in (
            ("fw1", [8, 128, 1024]), ("fb1", [1, 1024]),
            ("fw2", [8, 128, 512]), ("fb2", [1, 512]),
            ("fw3", [4, 128, 256]), ("fb3", [1, 256]),
            ("fw4", [2, 128, 1]), ("fb4", [1, 1]),
            ("gw1", [8, 128, 512]), ("gb1", [1, 512]),
            ("gw2", [4, 128, 256]), ("gb2", [1, 256]),
            ("gw3", [2, 128, 1]), ("gb3", [1, 1]),
            ("arange", [1, 1]))}
        self.out_t = nc.dram_tensor("out", [HID, TO], F32, kind="ExternalOutput").ap()
        self.psh = nc.dram_tensor("psh", [2, T, HID], F32, addr_space="Shared")
        self.x2buf = nc.dram_tensor("x2buf", [TO, HID], F32)
        self.shsum = nc.dram_tensor("shsum", [2, 1, HID], F32, addr_space="Shared")

        eye64 = np.eye(64, dtype=np.float32)
        I32p = np.concatenate([np.eye(32), np.zeros((32, 32))], 1)
        I32q = np.concatenate([np.zeros((32, 32)), np.eye(32)], 1)
        sel8 = np.zeros((8, 128), np.float32)
        for r in range(8):
            sel8[r, 64 * (r % 2):64 * (r % 2) + 64] = 1
        psel8 = np.zeros((8, 128), np.float32)
        for r in range(8):
            if r % 2 == 0:
                psel8[r, 0:32] = 1; psel8[r, 64:96] = 1
            else:
                psel8[r, 32:64] = 1; psel8[r, 96:128] = 1
        cn = dict(
            cI64=np.tile(eye64, (2, 1)).astype(BF),
            cIC1=np.tile(I32p, (4, 1)).astype(BF),
            cIC2=np.tile(I32q, (4, 1)).astype(BF),
            cUTlo=np.concatenate([np.triu(np.ones((64, 64))),
                                  np.zeros((64, 64))], 0).astype(BF),
            cUThi=np.concatenate([np.zeros((64, 64)),
                                  np.triu(np.ones((64, 64)))], 0).astype(BF),
            cMASK=np.tile(np.where(np.tril(np.ones((C, C)), -1).T > 0, 0.0,
                                   -1e30).astype(np.float32), (1, NCH)).astype(BF),
            cDUP64=np.tile(np.eye(64), (1, 2)).astype(BF),
            cSWAP=_swap32().astype(BF),
            cSEL8=sel8.astype(BF),
            cPSEL8=psel8.astype(BF),
            cONESC=np.ones((128, 1), np.float32).astype(BF),
            cONESR=np.ones((1, 128), np.float32).astype(BF),
            cONES8=np.ones((8, 1), np.float32).astype(BF),
            cI128=np.eye(128, dtype=np.float32).astype(BF),
        )
        for p in range(NP):
            m = np.zeros((128, 8), np.float32)
            m[0:64, 2 * p] = 1
            m[64:128, 2 * p + 1] = 1
            cn[f"cMS{p}"] = m.astype(BF)
            selp = np.zeros((8, 128), np.float32)
            for mm_ in range(128):
                selp[2 * p + (mm_ // 64), mm_] = 1
            cn[f"cSELP{p}"] = selp.astype(BF)
            pselp = np.zeros((8, 128), np.float32)
            for mm_ in range(128):
                pselp[2 * p + ((mm_ // 32) % 2), mm_] = 1
            cn[f"cPSELP{p}"] = pselp.astype(BF)
        self.consts_np = cn
        for fn_ in ("cI128", "cUTlo", "cUThi", "cDUP64", "cMASK", "cSELP0",
                    "cSELP1", "cSELP2", "cSELP3"):
            cn[fn_ + "f"] = cn[fn_].astype(np.float32)
        for dead in ("cUTlo", "cUThi", "cDUP64", "cMASK", "cIC1", "cIC2",
                     "cSEL8", "cPSEL8", "cPSELP0", "cPSELP1", "cPSELP2",
                     "cPSELP3", "cSELP0", "cSELP1", "cSELP2", "cSELP3",
                     "cSWAP" if False else "cONESC" if False else "cIC1"):
            cn.pop(dead, None)
        self.cap = {n: nc.inline_tensor(a, "ic_" + n).ap() for n, a in cn.items()}

        self.S = {n: es.enter_context(nc.semaphore("s_" + n))
                  for n in ("ld", "gld", "tp", "st", "T", "V", "A", "G", "SY")}
        self.PB = [es.enter_context(nc.psum_tensor(f"PB{i}", [128, 512], F32))
                   for i in range(8)]
        sb = lambda n, s, d=BF16: es.enter_context(nc.sbuf_tensor(n, s, d))
        self.cs = {n: sb("s_" + n, list(a.shape),
                         F32 if n.endswith("f") else BF16)
                   for n, a in cn.items()}

        self.cbias = sb("cbias", [128, 4], F32)
        for j, val in enumerate((1e-5, 1e-6, 1e-12)):
            nc.const_aps.register(F32, val, self.cbias[:, j:j + 1])

        self.xtok = [sb(f"xtok{i}", [128, HID], F32)
                     for i in range(max(self.NTW, 2))]
        self.xsq = sb("xsq", [128, HID], F32)
        self.xhatt = [sb(f"xhatt{i}", [128, HID]) for i in range(max(self.NTW, 2))]
        self.statc = sb("statc", [128, 16], F32)

        if "no_attn" not in self.debug:
            self._attn_phase()
        if "no_xf" not in self.debug:
            nc.all_core_barrier()
            self._x_phase()
            if "no_f" not in self.debug:
                nc.all_core_barrier()
                self._f_phase()
        es.close()
        return nc

    def _attn_phase(self):
        nc = self.nc
        T, W, NW, NCH, NTW = self.T, self.W, self.NW, self.NCH, self.NTW
        S, PB, cs = self.S, self.PB, self.cs
        ats = ExitStack()
        sb = lambda n, s, d=BF16: ats.enter_context(nc.sbuf_tensor(n, s, d))
        b = self.b = {}

        b["wa_s"] = sb("wa_s", [128, 8 * 2064])
        b["wo_s"] = sb("wo_s", [128, 4 * 1024])
        b["cw_s"] = sb("cw_s", [128, 48], F32)
        b["cb_s"] = sb("cb_s", [128, 12], F32)
        b["csa_s"] = sb("csa_s", [128, 2 * W])
        b["csb_s"] = sb("csb_s", [128, 2 * W])
        for n in self.small_in:
            b[n] = sb(n + "_s", [8, 1], F32)
        b["onorm_s"] = sb("onorm_s", [1, 64], F32)
        b["onormR"] = sb("onormR", [1, 512])
        b["abcolB"] = sb("abcolB", [8, 1], F32)
        b["onormT"] = sb("onormT", [128, 512], F32)
        b["negexpA"] = sb("negexpA", [8, 1], F32)
        b["abcol"] = sb("abcol", [8, 1], F32)

        b["xhf"] = sb("xhf", [128, 8 * W])
        for nm, n in (("qpre", NP), ("kpre", NP), ("vpre", NP)):
            b[nm] = [sb(f"{nm}{p}", [128, W + 3]) for p in range(n)]
        b["qcv"] = sb("qcv", [128, W])
        b["kcv"] = sb("kcv", [128, W])
        b["q2"] = [sb(f"q2_{p}", [128, W]) for p in range(NP)]
        b["k2"] = [sb(f"k2_{p}", [128, W]) for p in range(NP)]
        b["vfeat"] = [[sb(f"vfeat{j}_{p}", [128, W]) for p in range(NP)]
                      for j in range(2)]
        b["gfeat"] = [[sb(f"gfeat{j}_{p}", [128, W]) for p in range(NP)]
                      for j in range(2)]
        b["negk2"] = [sb(f"negk2_{q}", [128, W]) for q in range(2)]
        b["k2r"] = [sb(f"k2r{q}", [128, W]) for q in range(2)]
        b["k2s"] = [sb(f"k2s{q}", [128, W]) for q in range(2)]
        b["qt"] = [[sb(f"qt{j}_{p}", [128, W]) for p in range(NP)]
                   for j in range(2)]
        b["cnvt"] = [sb(f"cnvt{j}", [128, W], F32) for j in range(2)]
        b["rop1"] = sb("rop1", [128, W], F32)
        b["rop2"] = sb("rop2", [128, W], F32)
        b["sqt"] = [sb(f"sqt{j}", [128, W]) for j in range(3)]
        b["abzA"] = sb("abzA", [8, W], F32)
        b["abzB"] = sb("abzB", [8, W], F32)
        b["rscr"] = sb("rscr", [8, W], F32)
        b["rscr2"] = sb("rscr2", [8, W], F32)
        b["rscrB"] = sb("rscrB", [8, W], F32)
        b["rscr2B"] = sb("rscr2B", [8, W], F32)
        for n in ("la", "lnb", "LLk", "LLq", "g", "gprev", "rowKt",
                  "rowQt", "bj", "sK", "qks", "coef"):
            b["r_" + n] = sb("r_" + n, [8, W], F32)
        b["rb_coef"] = [sb(f"rb_coef{j}", [8, W]) for j in range(2)]
        b["gCrow"] = sb("gCrow", [8, NCH], F32)

        b["latok"] = sb("latok", [128, NTW * 8], F32)
        b["gtokb"] = sb("gtokb", [128, NTW * 8], F32)
        b["bjt"] = [sb(f"bjt{p}", [128, NCH], F32) for p in range(NP)]
        b["gCcol"] = [sb(f"gCcol{j}", [128, NCH * NP], F32) for j in range(2)]
        b["exb"] = [sb(f"exb{q}", [128, W], F32) for q in range(2)]
        b["Etmp"] = [sb(f"Etmp{q}", [128, W], F32) for q in range(2)]
        b["Eb"] = [sb(f"Eb{q}", [128, W]) for q in range(2)]
        b["NT"] = [sb(f"NT{p}", [128, W]) for p in range(NP)]
        b["LT"] = [[sb(f"LT{j}_{p}", [128, W]) for p in range(NP)]
                   for j in range(2)]
        b["ladT"] = [[sb(f"ladT{q}_{s}", [128, W]) for s in range(5)]
                     for q in range(2)]
        b["ladU"] = [[sb(f"ladU{q}_{s}", [128, W]) for s in range(4)]
                     for q in range(2)]
        b["NU"] = [b["ladU"][q][3] for q in range(2)]
        b["X"] = [sb(f"X{j}", [128, NP * NCH * 128]) for j in range(2)]
        b["Y1T"] = [[sb(f"Y1T{j}_{p}", [128, NCH * 64]) for p in range(NP)]
                    for j in range(2)]
        b["KT"] = [[sb(f"KT{j}_{p}", [128, NCH * 64]) for p in range(NP)]
                   for j in range(2)]
        b["Wcb"] = [sb(f"Wcb{q}", [128, NP * 64]) for q in range(2)]
        b["Sst"] = sb("Sst", [128, NP * 64], F32)
        b["Ssb"] = [sb(f"Ssb{q}", [128, NP * 64]) for q in range(2)]
        self.gch = 0
        b["otok"] = [sb(f"otok{i}", [128, 512], F32) for i in range(NTW)]
        b["vtok"] = [sb(f"vtok{i}", [128, 512]) for i in range(NTW)]
        b["gtokT"] = sb("gtokT", [128, NTW * 512])
        b["coefcol"] = sb("coefcol", [128, NTW * 8], F32)
        b["orst"] = sb("orst", [128, 8], F32)
        b["on2"] = sb("on2", [128, 512], F32)
        b["osil"] = sb("osil", [128, 512])
        b["o2f"] = sb("o2f", [128, 512])
        b["o2ff"] = sb("o2ff", [128, 512])
        self.GW = min(4, self.NW)
        b["pstage"] = sb("pstage", [128, self.GW * NTW * 1024], F32)

        with nc.Block() as blk:
            self.blk = blk
            t_wg = self.dma("g", [], lambda gp: [
                gp.dma_start(out=b["wa_s"][:].rearrange("p (a b) -> p a b", a=8),
                             in_=self.wa_in.rearrange("a p b -> p a b")),
                gp.dma_start(out=b["wo_s"][:].rearrange("p (a b) -> p a b", a=4),
                             in_=self.wo_in.rearrange("a p b -> p a b"))]
                + [gp.dma_start(out=b[n][:], in_=self.small_in[n])
                   for n in self.small_in]
                + [gp.dma_start(out=b["onorm_s"][:], in_=self.onorm_in)])
            t_ws = self.dma("s", [], lambda sy: [
                sy.dma_start(out=self.cs[n][:], in_=self.cap[n])
                for n in self.consts_np]
                + [sy.dma_start(out=b["cw_s"][:], in_=self.cw_in),
                   sy.dma_start(out=b["cb_s"][:], in_=self.cb_in)])
            self.wld = self.tok(S["ld"])
            self.wldg = t_wg

            self.dynv = {}
            def _snap_init(sy):
                pid = sy.partition_id()
                self.dynv["oslot"] = sy.snap(1 - pid % 2, min_val=0, max_val=1)
            self.blk.sync(_snap_init)
            self.war_cs = {}

            tA = self.step("a", [self.wld, self.wldg], lambda a:
                           a.activation(b["negexpA"][:], b["alog8"][:], AF.Exp))
            tV = self.step("v", [self.wld, self.wldg, tA], lambda v: (
                v.memset(self.cbias[:, 0:1], 1e-5),
                v.memset(self.cbias[:, 1:2], 1e-6),
                v.memset(self.cbias[:, 2:3], 1e-12),
                v.tensor_scalar_mul(b["negexpA"][:], b["negexpA"][:], -1.0),
                v.tensor_add(b["abcol"][0:8, :], b["bgk8"][:], b["dtb8"][:]),
                v.tensor_copy(b["abcolB"][:], b["bb8"][:]),
                v.tensor_copy(b["onormR"][0:1, :].rearrange("p (a b) -> p a b", a=8),
                              b["onorm_s"][:].rearrange("p (a b) -> p a b", a=1)
                              .broadcast_to([1, 8, 64])),
                [v.memset(b[nm][p][:, 0:3], 0.0)
                 for nm in ("qpre", "kpre", "vpre") for p in range(NP)],
                v.memset(b["Sst"][:], 0.0),
                v.memset(b["Ssb"][0][:], 0.0),
                v.memset(b["Ssb"][1][:], 0.0))[-1])
            tT = self.step("t", [tV], lambda t:
                           t.matmul(PB[0][:, 0:512], cs["cONESR"][:],
                                    b["onormR"][0:1, :], start=True, stop=True))
            tV2 = self.step("v", [tT], lambda v:
                            v.tensor_copy(b["onormT"][:], PB[0][:, 0:512]))
            self.pb_free[0] = tV2

            self.war_xtok = (S["A"], 0)
            self.war_xhf = (S["T"], 0)
            self.war_win = (S["V"], 0)
            for w in range(NW):
                self._window(w)
            self._chain_drain()

            self.blk.gpsimd(lambda gp: gp.wait_ge(S["st"], self.cnt.get(S["st"].name, 0)))
        ats.close()

    def _window(self, w):
        S, PB, cs, b = self.S, self.PB, self.cs, self.b
        T, W, NCH, NTW = self.T, self.W, self.NCH, self.NTW
        wo = w * W
        wp = w % 2
        st = self.step
        xtok, xsq, xhatt, statc = self.xtok, self.xsq, self.xhatt, self.statc

        t_cs = self.dma("s", [self.war_cs.get(w % 2, (S["V"], 0))], lambda sy: [
            sy.dma_start(out=b["csa_s"][:, (w % 2) * W:(w % 2 + 1) * W],
                         in_=self.csa_in[:, wo:wo + W]),
            sy.dma_start(out=b["csb_s"][:, (w % 2) * W:(w % 2 + 1) * W],
                         in_=self.csb_in[:, wo:wo + W])])
        self.t_cs = t_cs
        t_x = self.dma("s", [self.war_xtok], lambda sy: [
            sy.dma_start(out=xtok[i][:],
                         in_=self.x_in[wo + 128 * i: wo + 128 * (i + 1), :])
            for i in range(NTW)])
        t_app = None
        for i in range(NTW):
            c = statc
            tv = st("v", [t_x], lambda v, i=i: (
                v.tensor_mul(self.xsq[:], xtok[i][:], xtok[i][:]),
                v.reduce_sum(c[:, 4 * i:4 * i + 1], xtok[i][:], AX.X),
                v.reduce_sum(c[:, 4 * i + 1:4 * i + 2], self.xsq[:],
                             AX.X),
                v.tensor_scalar_mul(c[:, 4 * i:4 * i + 1],
                                    c[:, 4 * i:4 * i + 1], 1.0 / HID),
                v.tensor_scalar_mul(c[:, 4 * i + 1:4 * i + 2],
                                    c[:, 4 * i + 1:4 * i + 2], 1.0 / HID),
                v.scalar_tensor_tensor(c[:, 4 * i + 2:4 * i + 3],
                                       c[:, 4 * i:4 * i + 1],
                                       c[:, 4 * i:4 * i + 1],
                                       c[:, 4 * i + 1:4 * i + 2],
                                       ALU.mult, ALU.subtract))[-1])
            ta_ = st("a", [tv], lambda a, i=i:
                     a.activation(c[:, 4 * i + 3:4 * i + 4],
                                  c[:, 4 * i + 2:4 * i + 3], AF.Sqrt,
                                  bias=1e-5, scale=-1.0))
            t_app = st("v", [ta_, getattr(self, "war_xhatt", (S["tp"], 0))],
                       lambda v, i=i: (
                       v.reciprocal(c[:, 4 * i + 3:4 * i + 4],
                                    c[:, 4 * i + 3:4 * i + 4]),
                       v.tensor_scalar(xhatt[i][:], xtok[i][:],
                                       c[:, 4 * i:4 * i + 1],
                                       c[:, 4 * i + 3:4 * i + 4],
                                       ALU.subtract, ALU.mult))[-1])
        self.war_xtok = t_app

        t_tp = self.step("s", [t_app, self.war_xhf, self.tok(S["tp"])], lambda sy: [
            self._tp(sy, b["xhf"][:].rearrange("p (c t) -> p c t", c=8)
                     [:, :, 128 * i:128 * (i + 1)], xhatt[i][:])
            for i in range(NTW)][-1] and None, dma=True)
        t_tp = self.tok(S["tp"])
        self.war_xhatt = t_tp
        self._chain_step(4)
        if "s1" in self.debug:
            return

        dsts = ([("qkv", b["qpre"][p]) for p in range(NP)]
                + [("qkv", b["kpre"][p]) for p in range(NP)]
                + [("qkv", b["vpre"][p]) for p in range(NP)]
                + [("g", b["gfeat"][wp][p]) for p in range(NP)]
                + [("abA", b["abzA"]), ("abB", b["abzB"])])
        for ot in range(18):
            bk = 4 + ot % 2
            waits = [self.bank(bk)]
            if ot == 0:
                ww = self.war_win if isinstance(self.war_win, list) \
                    else [self.war_win]
                waits += [t_tp, self.wld, self.wldg] + ww
            M = 128 if ot < 16 else 8
            co = 2048 if ot < 16 else (2048 + 8 * (ot - 16))
            tmm = st("t", waits, lambda t, ot=ot, bk=bk, M=M: [
                t.matmul(PB[bk][0:M, 0:W],
                         b["wa_s"][:, dc * 2064 + (ot * 128 if ot < 16 else
                                                   2048 + 8 * (ot - 16)):
                                   dc * 2064 + (ot * 128 if ot < 16 else
                                                2048 + 8 * (ot - 16)) + M],
                         b["xhf"][:, dc * W:(dc + 1) * W],
                         start=(dc == 0), stop=(dc == 7))
                for dc in range(8)][-1])
            kind, dst = dsts[ot]
            tev = st("v", [tmm], lambda v, kind=kind, dst=dst, bk=bk, M=M:
                     v.tensor_copy(dst[:, 3:3 + W] if kind == "qkv"
                                   else dst[0:M, 0:W],
                                   PB[bk][0:M, 0:W]))
            self.pb_free[bk] = tev
        t_proj = tev
        self.war_xhf = self.tok(S["T"])
        self._chain_step(3)
        if "s2" in self.debug:
            return

        for p in range(NP):
            for qi in range(3):
                src = (b["qpre"], b["kpre"], b["vpre"])[qi][p]
                dst = (b["qcv"], b["kcv"], b["vfeat"][wp][p])[qi]
                base = qi * 16 + p * 4
                cj = (qi * NP + p) % 2
                wcn = getattr(self, "war_cnvt", {})
                self.war_cnvt = wcn
                tc = st("v", [t_proj, wcn.get(cj, (S["A"], 0))],
                        lambda v, src=src, base=base, cj=cj: (
                    v.tensor_scalar_mul(b["cnvt"][cj][:, 0:W], src[:, 0:W],
                                        b["cw_s"][:, base:base + 1]),
                    [v.scalar_tensor_tensor(b["cnvt"][cj][:, 0:W],
                                            src[:, tap:tap + W],
                                            b["cw_s"][:, base + tap:base + tap + 1],
                                            b["cnvt"][cj][:, 0:W],
                                            ALU.mult, ALU.add)
                     for tap in (1, 2, 3)],
                    v.tensor_copy(src[:, 0:3], src[:, W:W + 3]))[-1])
                war_cv = getattr(self, "war_cv", {})
                tsl = st("a", [tc, war_cv.get(qi, (S["T"], 0))],
                         lambda a, qi=qi, p=p, dst=dst, cj=cj:
                         a.activation(dst[:, 0:W], b["cnvt"][cj][:, 0:W],
                                      AF.Silu,
                                      bias=b["cb_s"][:, qi * 4 + p:qi * 4 + p + 1]))
                wcn[cj] = tsl
                if qi < 2:
                    rsrc = b["qcv"] if qi == 0 else b["kcv"]
                    rdst = (b["q2"] if qi == 0 else b["k2"])[p]
                    bk = 6
                    tsw = st("t", [tsl, self.bank(bk)], lambda t, rsrc=rsrc, bk=bk:
                             t.matmul(PB[bk][:, 0:W], cs["cSWAP"][:],
                                      rsrc[:, 0:W], start=True, stop=True))
                    cso = (w % 2) * W
                    trp = st("v", [tsw, self.t_cs],
                             lambda v, rsrc=rsrc, rdst=rdst, bk=bk, cso=cso: (
                        v.tensor_mul(b["rop1"][:, 0:W], PB[bk][:, 0:W],
                                     b["csb_s"][:, cso:cso + W]),
                        v.tensor_mul(b["rop2"][:, 0:W], rsrc[:, 0:W],
                                     b["csa_s"][:, cso:cso + W]),
                        v.tensor_add(rdst[:, 0:W], b["rop2"][:, 0:W],
                                     b["rop1"][:, 0:W]))[-1])
                    self.pb_free[bk] = trp
                    war_cv[qi] = trp
                    self.war_cv = war_cv
        t_qkv = self.tok(S["V"])
        t_silu = self.tok(S["A"])
        self.war_cs[w % 2] = t_qkv
        self._chain_step(2)
        if "s4" in self.debug:
            return

        r = lambda n: b["r_" + n]
        rb = lambda n: b["rb_" + n]
        t1 = st("v", [t_proj], lambda v: (
            v.tensor_scalar_add(b["rscr"][:, 0:W], b["abzA"][:, 0:W],
                                b["abcol"][:, 0:1]),
            v.tensor_scalar_add(b["rscrB"][:, 0:W], b["abzB"][:, 0:W],
                                b["abcolB"][:, 0:1]))[-1])
        t2 = st("a", [t1], lambda a: (
            a.activation(b["rscr2"][:, 0:W], b["rscr"][:, 0:W], AF.Abs),
            a.activation(b["rscr2B"][:, 0:W], b["rscrB"][:, 0:W], AF.Abs),
            a.activation(b["rscr2"][:, 0:W], b["rscr2"][:, 0:W], AF.Exp,
                         scale=-1.0),
            a.activation(b["rscr2B"][:, 0:W], b["rscr2B"][:, 0:W], AF.Exp,
                         scale=-1.0),
            a.activation(b["rscr2"][:, 0:W], b["rscr2"][:, 0:W], AF.Ln,
                         bias=1.0),
            a.activation(b["rscr2B"][:, 0:W], b["rscr2B"][:, 0:W], AF.Ln,
                         bias=1.0))[-1])
        t3a = st("v", [t2], lambda v: (
            v.tensor_scalar_max(r("la")[:, 0:W], b["rscr"][:, 0:W], 0.0),
            v.tensor_add(r("la")[:, 0:W], r("la")[:, 0:W], b["rscr2"][:, 0:W]),
            v.tensor_scalar_mul(r("la")[:, 0:W], r("la")[:, 0:W],
                                b["negexpA"][:, 0:1]))[-1])
        t3 = st("v", [t2], lambda v: (
            v.tensor_scalar_min(r("lnb")[:, 0:W], b["rscrB"][:, 0:W], 0.0),
            v.tensor_sub(r("lnb")[:, 0:W], r("lnb")[:, 0:W],
                         b["rscr2B"][:, 0:W]))[-1])

        t_sums = []
        sums = (("LLk", "k2", "k2"), ("LLq", "q2", "q2"), ("qks", "q2", "k2"))
        tsqd = {}
        tmm = {j: None for j in range(3)}
        for p in range(NP):
            for j, (nm, aa, bb_) in enumerate(sums):
                tsqd[(j, p)] = st("v", [t_qkv]
                                  + ([tmm[j]] if tmm[j] else []),
                                  lambda v, nm=nm, aa=aa, bb_=bb_, p=p, j=j:
                                  v.tensor_mul(b["sqt"][j][:, 0:W],
                                               b[aa][p][:, 0:W],
                                               b[bb_][p][:, 0:W]))
            for j in range(3):
                tmm[j] = st("t", [tsqd[(j, p)]]
                            + ([self.bank(4 + j)] if p == 0 else []),
                            lambda t, p=p, j=j:
                            t.matmul(PB[4 + j][0:8, 0:W], cs[f"cMS{p}"][:],
                                     b["sqt"][j][:, 0:W], start=(p == 0),
                                     stop=(p == 3)))
        for j, (nm, aa, bb_) in enumerate(sums):
            if nm == "qks":
                tl = st("v", [tmm[j]], lambda v, nm=nm, j=j:
                        v.tensor_copy(r(nm)[:, 0:W], PB[4 + j][0:8, 0:W]))
            else:
                tl = st("a", [tmm[j]], lambda a, nm=nm, j=j:
                        a.activation(r(nm)[:, 0:W], PB[4 + j][0:8, 0:W],
                                     AF.Ln, bias=1e-12))
            self.pb_free[4 + j] = tl
            t_sums.append(tl)

        t4 = st("t", [t3, t3a, self.bank(7)], lambda t: [
            t.matmul(PB[7][:, 8 * i:8 * i + 8],
                     r("la")[:, 128 * i:128 * (i + 1)],
                     cs["cI128f"][0:8, 0:8], start=(i == 0), stop=(i == NTW - 1))
            for i in range(NTW)][-1])
        t5 = st("v", [t4], lambda v:
                v.tensor_copy(b["latok"][:, 0:NTW * 8], PB[7][:, 0:NTW * 8]))
        self.pb_free[7] = t5
        t6 = st("t", [t5, self.bank(7)], lambda t: [
            t.matmul(PB[7][64 * u:64 * u + 64, 8 * i:8 * i + 8],
                     cs["cUTlof" if u == 0 else "cUThif"][:],
                     b["latok"][:, 8 * i:8 * i + 8],
                     start=(i == 0), stop=(i == NTW - 1),
                     tile_position=(0, 64 * u))
            for u in range(2) for i in range(NTW)][-1])
        t7 = st("v", [t6], lambda v:
                v.tensor_copy(b["gtokb"][:, 0:NTW * 8], PB[7][:, 0:NTW * 8]))
        self.pb_free[7] = t7
        t8 = st("t", [t7, self.bank(7)], lambda t: [
            t.matmul(PB[7][0:8, 128 * i:128 * (i + 1)],
                     b["gtokb"][:, 8 * i:8 * i + 8], cs["cI128f"][:],
                     start=(i == 0), stop=(i == NTW - 1))
            for i in range(NTW)][-1])
        t9 = st("v", [t8] + t_sums, lambda v: (
            v.tensor_copy(r("g")[:, 0:W], PB[7][0:8, 0:W]),
            v.tensor_sub(r("gprev")[:, 0:W], r("g")[:, 0:W], r("la")[:, 0:W]),
            v.tensor_copy(b["gCrow"][:].rearrange("p (c o) -> p c o", o=1),
                          r("g")[:, 0:W].rearrange("p (c t) -> p c t", t=64)
                          [:, :, 63:64]),

            v.scalar_tensor_tensor(r("rowKt")[:, 0:W], r("LLk")[:, 0:W], -0.5,
                                   r("g")[:, 0:W], ALU.mult, ALU.add),
            v.scalar_tensor_tensor(r("rowQt")[:, 0:W], r("LLq")[:, 0:W], -0.5,
                                   r("gprev")[:, 0:W], ALU.mult, ALU.add),
            v.scalar_tensor_tensor(r("bj")[:, 0:W], r("g")[:, 0:W], -1.0,
                                   r("lnb")[:, 0:W], ALU.mult, ALU.add),
            v.scalar_tensor_tensor(r("bj")[:, 0:W], r("LLk")[:, 0:W], -0.5,
                                   r("bj")[:, 0:W], ALU.mult, ALU.add),
            v.tensor_add(r("sK")[:, 0:W].rearrange("p (c t) -> p c t", t=64),
                         r("bj")[:, 0:W].rearrange("p (c t) -> p c t", t=64),
                         b["gCrow"][:].rearrange("p (c o) -> p c o", o=1)
                         .broadcast_to([8, NCH, 64])),
            v.tensor_add(r("coef")[:, 0:W], r("LLq")[:, 0:W], r("LLk")[:, 0:W]),
            v.tensor_scalar_mul(r("coef")[:, 0:W], r("coef")[:, 0:W], -0.5))[-1])
        self.pb_free[7] = t9
        t10 = st("a", [t9], lambda a:
                 a.activation(b["rscr2"][:, 0:W], r("coef")[:, 0:W], AF.Exp))
        t11 = st("v", [t10], lambda v: (
            v.tensor_mul(r("coef")[:, 0:W], r("qks")[:, 0:W], b["rscr2"][:, 0:W]),
            v.tensor_scalar_mul(r("coef")[:, 0:W], r("coef")[:, 0:W],
                                b["dp8"][:, 0:1]),
            v.tensor_copy(b["rb_coef"][w % 2][:, 0:W], r("coef")[:, 0:W]))[-1])

        t12 = st("t", [t11, self.bank(5)], lambda t: [
            t.matmul(PB[5][64 * u:64 * u + 64, p * NCH + cc:p * NCH + cc + 1],
                     r("bj")[0:8, cc * 64:(cc + 1) * 64],
                     cs["cI128f"][0:8, 2 * p + u:2 * p + u + 1],
                     start=(p == 0 and cc == 0),
                     stop=(p == NP - 1 and cc == NCH - 1),
                     tile_position=(0, 64 * u))
            for u in range(2) for p in range(NP) for cc in range(NCH)][-1])
        t13 = st("v", [t12], lambda v: [
            v.tensor_copy(b["bjt"][p][:], PB[5][:, p * NCH:(p + 1) * NCH])
            for p in range(NP)][-1])
        self.pb_free[5] = t13
        t14 = st("t", [t11, self.bank(6)], lambda t: [
            t.matmul(PB[6][:, cc * NP + p:cc * NP + p + 1],
                     cs[f"cSELP{p}f"][:],
                     b["gCrow"][0:8, cc:cc + 1],
                     start=(cc == 0 and p == 0),
                     stop=(cc == NCH - 1 and p == NP - 1))
            for cc in range(NCH) for p in range(NP)][-1])
        t15 = st("a", [t14], lambda a:
                 a.activation(b["gCcol"][wp][:, 0:NCH * NP],
                              PB[6][:, 0:NCH * NP], AF.Exp))
        self.pb_free[6] = t15
        self._chain_step(1)
        if "s5" in self.debug:
            return

        BT = (4, 6)
        BU = (5, 7)
        pair_tok = [None] * NP
        wx = getattr(self, "war_px", {})
        self.war_px = wx
        NRND = (NCH + 3) // 4
        for gdx in range(NP // 2):
            ps = (2 * gdx, 2 * gdx + 1)
            Q = {p: p % 2 for p in ps}
            wq = {p: wx.get(Q[p], (S["V"], 0)) for p in ps}

            tks = {p: [] for p in ps}
            for rowname, dname in (("rowKt", "k2r"), ("sK", "k2s"),
                                   ("rowQt", "qt")):
                tbm = {}
                for p in ps:
                    q = Q[p]
                    tbm[p] = st("t", [t9, self.bank(BU[q])],
                                lambda t, rowname=rowname, p=p, q=q:
                        t.matmul(PB[BU[q]][:, 0:W], cs[f"cSELP{p}f"][:],
                                 r(rowname)[0:8, 0:W], start=True, stop=True))
                tex = {}
                for p in ps:
                    q = Q[p]
                    tex[p] = st("a", [tbm[p], wq[p],
                                      wx.get(("exb", q), (S["V"], 0))],
                                lambda a, q=q:
                                a.activation(b["exb"][q][:, 0:W],
                                             PB[BU[q]][:, 0:W], AF.Exp))
                    self.pb_free[BU[q]] = tex[p]
                for p in ps:
                    q = Q[p]
                    dst = b["qt"][wp][p] if dname == "qt" else b[dname][q]
                    srcs = "q2" if rowname == "rowQt" else "k2"
                    tm = st("v", [tex[p], t_qkv, wq[p]],
                            lambda v, dst=dst, srcs=srcs, p=p, q=q:
                            v.tensor_mul(dst[:, 0:W], b[srcs][p][:, 0:W],
                                         b["exb"][q][:, 0:W]))
                    wx[("exb", q)] = tm
                    tks[p].append(tm)
            t_neg = {}
            for p in ps:
                q = Q[p]
                t_neg[p] = st("v", [t_qkv, wq[p]],
                              lambda v, p=p, q=q:
                              v.tensor_scalar_mul(b["negk2"][q][:, 0:W],
                                                  b["k2"][p][:, 0:W], -1.0))

            t_gram = {}
            t_nb = {p: None for p in ps}
            for rowname, dstl, grhs in (("rowKt", "NT", "k2"),
                                        ("rowQt", "LT", "q2")):
                for p in ps:
                    q = Q[p]
                    t_gram[p] = st("t", [t_neg[p], self.bank(BT[q])]
                                   + ([t_nb[p]] if t_nb[p] else []),
                                   lambda t, p=p, q=q, grhs=grhs: [
                        t.matmul(PB[BT[q]][64 * u:64 * u + 64,
                                           cc * 64:(cc + 1) * 64],
                                 b["negk2"][q][64 * u:64 * u + 64,
                                               cc * 64:(cc + 1) * 64],
                                 b[grhs][p][64 * u:64 * u + 64,
                                            cc * 64:(cc + 1) * 64],
                                 start=(cc == 0), stop=(cc == NCH - 1),
                                 tile_position=(64 * u, 64 * u))
                        for u in range(2) for cc in range(NCH)][-1])
                ta_ = {}
                for p in ps:
                    q = Q[p]
                    ta_[p] = st("t", [t9, self.bank(BU[q])],
                                lambda t, rowname=rowname, p=p, q=q: (
                        t.matmul(PB[BU[q]][:, 0:W], cs[f"cSELP{p}f"][:],
                                 r(rowname)[0:8, 0:W], start=True, stop=False),
                        t.matmul(PB[BU[q]][:, 0:W], cs["cDUP64f"][:],
                                 cs["cMASKf"][:, 0:W],
                                 start=False, stop=True))[-1])
                tt1 = {}
                for p in ps:
                    q = Q[p]
                    tt1[p] = st("v", [ta_[p], wq[p],
                                      wx.get(("Etmp", q), (S["A"], 0))],
                                lambda v, p=p, q=q:
                                v.tensor_add(b["Etmp"][q][:, 0:W].rearrange(
                                    "p (c t) -> p c t", t=64),
                                    PB[BU[q]][:, 0:W].rearrange(
                                        "p (c t) -> p c t", t=64),
                                    b["bjt"][p][:].rearrange(
                                        "p (c o) -> p c o", o=1)
                                    .broadcast_to([128, NCH, 64])))
                    self.pb_free[BU[q]] = tt1[p]
                texp = {}
                for p in ps:
                    q = Q[p]
                    texp[p] = st("a", [tt1[p], wx.get(("Eb", q), (S["V"], 0))],
                                 lambda a, q=q:
                                 a.activation(b["Eb"][q][:, 0:W],
                                              b["Etmp"][q][:, 0:W], AF.Exp))
                    wx[("Etmp", q)] = texp[p]
                for p in ps:
                    q = Q[p]
                    t_nb[p] = st("v", [texp[p], t_gram[p]],
                                 lambda v, dstl=dstl, p=p, q=q:
                                 v.tensor_mul(
                                     (b["LT"][wp] if dstl == "LT"
                                      else b["NT"])[p][:, 0:W],
                                     b["Eb"][q][:, 0:W],
                                     PB[BT[q]][:, 0:W]))
                    wx[("Eb", q)] = t_nb[p]
                    self.pb_free[BT[q]] = t_nb[p]

            t_nu = {}
            for p in ps:
                q = Q[p]
                tmm = st("t", [t_nb[p], self.bank(BT[q])],
                         lambda t, p=p, q=q: [
                    t.matmul(PB[BT[q]][64 * u:64 * u + 64,
                                       cc * 64:(cc + 1) * 64],
                             b["NT"][p][64 * u:64 * u + 64,
                                        cc * 64:(cc + 1) * 64],
                             cs["cI64"][64 * u:64 * u + 64, :],
                             start=(cc == 0), stop=(cc == NCH - 1),
                             tile_position=(64 * u, 64 * u))
                    for u in range(2) for cc in range(NCH)][-1])
                t_nu[p] = st("v", [tmm, wq[p]], lambda v, q=q:
                             v.tensor_copy(b["NU"][q][:, 0:W],
                                           PB[BT[q]][:, 0:W]))
                self.pb_free[BT[q]] = t_nu[p]

            curT = {p: b["NT"][p] for p in ps}
            curU = {p: b["NU"][Q[p]] for p in ps}
            tTprev = {p: t_nb[p] for p in ps}
            tUprev = dict(t_nu)
            for s in range(1, 6):
                tlt = {}
                for p in ps:
                    q = Q[p]
                    tlt[p] = st("t", [tTprev[p], tUprev[p], self.bank(BT[q])],
                                lambda t, cT=curT[p], cU=curU[p], q=q: [
                        t.matmul(PB[BT[q]][64 * u:64 * u + 64,
                                           cc * 64:(cc + 1) * 64],
                                 cU[64 * u:64 * u + 64, cc * 64:(cc + 1) * 64],
                                 cT[64 * u:64 * u + 64, cc * 64:(cc + 1) * 64],
                                 start=(cc == 0), stop=(cc == NCH - 1),
                                 tile_position=(64 * u, 64 * u))
                        for u in range(2) for cc in range(NCH)][-1])
                tlu = {}
                if s <= 4:
                    for p in ps:
                        q = Q[p]
                        tlu[p] = st("t", [tTprev[p], tUprev[p],
                                          self.bank(BU[q])],
                                    lambda t, cT=curT[p], cU=curU[p], q=q: [
                            t.matmul(PB[BU[q]][64 * u:64 * u + 64,
                                               cc * 64:(cc + 1) * 64],
                                     cT[64 * u:64 * u + 64,
                                        cc * 64:(cc + 1) * 64],
                                     cU[64 * u:64 * u + 64,
                                        cc * 64:(cc + 1) * 64],
                                     start=(cc == 0), stop=(cc == NCH - 1),
                                     tile_position=(64 * u, 64 * u))
                            for u in range(2) for cc in range(NCH)][-1])
                for p in ps:
                    q = Q[p]
                    tTe = st("a", [tlt[p], wq[p]], lambda a, s=s, q=q:
                             a.activation(b["ladT"][q][s - 1][:, 0:W],
                                          PB[BT[q]][:, 0:W], AF.Copy))
                    self.pb_free[BT[q]] = tTe
                    curT[p], tTprev[p] = b["ladT"][q][s - 1], tTe
                if s <= 4:
                    for p in ps:
                        q = Q[p]
                        tUe = st("v", [tlu[p], wq[p]], lambda v, s=s, q=q:
                                 v.tensor_copy(b["ladU"][q][s - 1][:, 0:W],
                                               PB[BU[q]][:, 0:W]))
                        self.pb_free[BU[q]] = tUe
                        curU[p], tUprev[p] = b["ladU"][q][s - 1], tUe

            t_x = {}
            for rnd in range(NRND):
                ccs = list(range(4 * rnd, min(4 * rnd + 4, NCH)))
                trh = {}
                for p in ps:
                    q = Q[p]
                    trh[p] = st("t", [tks[p][0], t_qkv, self.bank(BT[q])],
                                lambda t, ccs=ccs, p=p, q=q: [
                        mm for u in range(2) for cc in ccs for mm in (
                            t.matmul(PB[BT[q]][64 * u:64 * u + 64,
                                               (cc - ccs[0]) * 128:
                                               (cc - ccs[0]) * 128 + 64],
                                     b["k2r"][q][64 * u:64 * u + 64,
                                                 cc * 64:(cc + 1) * 64],
                                     cs["cI64"][64 * u:64 * u + 64, :],
                                     start=(cc == ccs[0]), stop=False,
                                     tile_position=(64 * u, 64 * u)),
                            t.matmul(PB[BT[q]][64 * u:64 * u + 64,
                                               (cc - ccs[0]) * 128 + 64:
                                               (cc - ccs[0]) * 128 + 128],
                                     b["vfeat"][wp][p][64 * u:64 * u + 64,
                                                       cc * 64:(cc + 1) * 64],
                                     cs["cI64"][64 * u:64 * u + 64, :],
                                     start=False, stop=(cc == ccs[-1]),
                                     tile_position=(64 * u, 64 * u)))][-1])
                for p in ps:
                    q = Q[p]
                    xo = p * NCH * 128
                    t_x[p] = st("v", [trh[p]], lambda v, ccs=ccs, xo=xo, q=q:
                                v.tensor_copy(
                                    b["X"][wp][:, xo + ccs[0] * 128:
                                               xo + (ccs[-1] + 1) * 128],
                                    PB[BT[q]][:, 0:len(ccs) * 128]))
                    self.pb_free[BT[q]] = t_x[p]

            for s in range(5, -1, -1):
                for rnd in range(NRND):
                    ccs = list(range(4 * rnd, min(4 * rnd + 4, NCH)))
                    tap = {}
                    for p in ps:
                        q = Q[p]
                        tiles = b["ladT"][q][s - 1] if s >= 1 else b["NT"][p]
                        xo = p * NCH * 128
                        tap[p] = st("t", [t_x[p], self.bank(BT[q])],
                                    lambda t, tiles=tiles, ccs=ccs, xo=xo,
                                    q=q: [
                            t.matmul(PB[BT[q]][64 * u:64 * u + 64,
                                               (cc - ccs[0]) * 128:
                                               (cc - ccs[0]) * 128 + 128],
                                     tiles[64 * u:64 * u + 64,
                                           cc * 64:(cc + 1) * 64],
                                     b["X"][wp][64 * u:64 * u + 64,
                                                xo + cc * 128:
                                                xo + cc * 128 + 128],
                                     start=(cc == ccs[0]), stop=(cc == ccs[-1]),
                                     tile_position=(64 * u, 64 * u))
                            for u in range(2) for cc in ccs][-1])
                    for p in ps:
                        q = Q[p]
                        xo = p * NCH * 128
                        t_x[p] = st("v", [tap[p]], lambda v, ccs=ccs, xo=xo,
                                    q=q:
                                    v.tensor_add(
                                        b["X"][wp][:, xo + ccs[0] * 128:
                                                   xo + (ccs[-1] + 1) * 128],
                                        b["X"][wp][:, xo + ccs[0] * 128:
                                                   xo + (ccs[-1] + 1) * 128],
                                        PB[BT[q]][:, 0:len(ccs) * 128]))
                        self.pb_free[BT[q]] = t_x[p]

            ty1 = {}
            tkt = {}
            for p in ps:
                q = Q[p]
                xo = p * NCH * 128
                tym = st("t", [t_x[p], self.bank(BT[q])],
                         lambda t, xo=xo, q=q: [
                    t.matmul(PB[BT[q]][64 * u:64 * u + 64,
                                       cc * 64:(cc + 1) * 64],
                             b["X"][wp][64 * u:64 * u + 64,
                                        xo + cc * 128: xo + cc * 128 + 64],
                             cs["cI64"][64 * u:64 * u + 64, :],
                             start=(cc == 0), stop=(cc == NCH - 1),
                             tile_position=(64 * u, 64 * u))
                    for u in range(2) for cc in range(NCH)][-1])
                ty1[p] = st("v", [tym, wq[p]], lambda v, p=p, q=q:
                            v.tensor_copy(b["Y1T"][wp][p][:, 0:NCH * 64],
                                          PB[BT[q]][:, 0:NCH * 64]))
                self.pb_free[BT[q]] = ty1[p]
            for p in ps:
                q = Q[p]
                tkm = st("t", [tks[p][1], self.bank(BU[q])],
                         lambda t, p=p, q=q: [
                    t.matmul(PB[BU[q]][64 * u:64 * u + 64,
                                       cc * 64:(cc + 1) * 64],
                             b["k2s"][q][64 * u:64 * u + 64,
                                         cc * 64:(cc + 1) * 64],
                             cs["cI64"][64 * u:64 * u + 64, :],
                             start=(cc == 0), stop=(cc == NCH - 1),
                             tile_position=(64 * u, 64 * u))
                    for u in range(2) for cc in range(NCH)][-1])
                tkt[p] = st("v", [tkm, wq[p]], lambda v, p=p, q=q:
                            v.tensor_copy(b["KT"][wp][p][:, 0:NCH * 64],
                                          PB[BU[q]][:, 0:NCH * 64]))
                self.pb_free[BU[q]] = tkt[p]
            for p in ps:
                pair_tok[p] = (t_x[p], ty1[p], tkt[p])
                wx[Q[p]] = tkt[p]
            self._chain_step(2)

        self._chain_drain()
        self.war_win = [self.tok(S["V"]), self.tok(S["T"]), self.tok(S["A"])]
        if "s6" not in self.debug:
            self.pending_chain = self._chain_gen(w, pair_tok, t_qkv, t11)

    def _chain_step(self, n=1):
        g = getattr(self, "pending_chain", None)
        if g is None:
            return
        for _ in range(n):
            try:
                next(g)
            except StopIteration:
                self.pending_chain = None
                return

    def _chain_drain(self):
        g = getattr(self, "pending_chain", None)
        if g is None:
            return
        for _ in g:
            pass
        self.pending_chain = None

    def _tp(self, sy, out_ap, in_ap):
        ins = sy.dma_start(out=out_ap, in_=in_ap, transpose=True)
        self._inc(ins, self.S["tp"], 16)
        return ins

    def _chain_gen(self, w, pair_tok, t_qkv, t11):
        S, PB, cs, b = self.S, self.PB, self.cs, self.b
        W, NCH, NTW = self.W, self.NCH, self.NTW
        wo = w * W
        wp = w % 2
        st = self.step

        for i in range(NTW):
            tvm = st("t", [t_qkv, self.bank(2)], lambda t, i=i: [
                t.matmul(PB[2][(cc % 2) * 64:(cc % 2) * 64 + 64,
                               p * 128 + u * 64: p * 128 + u * 64 + 64],
                         b["vfeat"][wp][p][64 * u:64 * u + 64,
                                           cc * 64:(cc + 1) * 64],
                         cs["cI64"][64 * u:64 * u + 64, :],
                         start=True, stop=True,
                         tile_position=(64 * u, (cc % 2) * 64))
                for p in range(NP) for u in range(2)
                for cc in (2 * i, 2 * i + 1)][-1])
            tve = st("v", [tvm], lambda v, i=i:
                     v.tensor_copy(b["vtok"][i][:], PB[2][:, 0:512]))
            self.pb_free[2] = tve
        t_vtok = self.tok(S["V"])

        def g_tp(sy):
            for p in range(NP):
                ins = sy.dma_start(
                    out=b["gtokT"][:].rearrange("p (i f) -> p i f", f=512)
                    [:, :, p * 128:(p + 1) * 128],
                    in_=b["gfeat"][wp][p][:, 0:W], transpose=True)
                self._inc(ins, S["tp"], 16)
        tgt0 = self.tok(S["tp"])
        _tpf = self.tok(S["tp"])
        self.blk.sync(lambda sy: (self._wait(sy, _tpf),
                                  self._wait(sy, t_qkv), g_tp(sy)))
        t_gt = self.tok(S["tp"])

        tcm = st("t", [t11, self.bank(3)], lambda t: [
            t.matmul(PB[3][:, 8 * i:8 * i + 8],
                     b["rb_coef"][wp][:, 128 * i:128 * (i + 1)],
                     cs["cI128"][0:8, 0:8], start=(i == 0), stop=(i == NTW - 1))
            for i in range(NTW)][-1])
        tcc = st("v", [tcm], lambda v:
                 v.tensor_copy(b["coefcol"][:, 0:NTW * 8], PB[3][:, 0:NTW * 8]))
        self.pb_free[3] = tcc
        yield

        t_wc = {}
        t_o = t_s = None
        pend = None

        def emit_om(ccp, gcp):
            par = (ccp % 2) * 64
            bk = 2 + (ccp % 2)
            t_omp = st("t", [t_wc[gcp % 2], self.bank(bk)],
                       lambda t, ccp=ccp, gcp=gcp, par=par, bk=bk: [
                (t.matmul(PB[bk][par:par + 64,
                                 p * 128 + u * 64: p * 128 + u * 64 + 64],
                          b["qt"][wp][p][64 * u:64 * u + 64,
                                         ccp * 64:(ccp + 1) * 64],
                          b["Ssb"][(gcp - 1) % 2][64 * u:64 * u + 64,
                                                  p * 64:(p + 1) * 64],
                          start=(p == 0), stop=False,
                          tile_position=(64 * u, par))
                 if part == 0 else
                 t.matmul(PB[bk][par:par + 64,
                                 p * 128 + u * 64: p * 128 + u * 64 + 64],
                          b["LT"][wp][p][64 * u:64 * u + 64,
                                         ccp * 64:(ccp + 1) * 64],
                          b["Wcb"][gcp % 2][64 * u:64 * u + 64,
                                            p * 64:(p + 1) * 64],
                          start=False, stop=(p == NP - 1),
                          tile_position=(64 * u, par)))
                for u in range(2) for part in (0, 1) for p in range(NP)][-1])
            t_op = st("v", [t_omp], lambda v, ccp=ccp, par=par, bk=bk:
                      v.tensor_copy(b["otok"][ccp // 2][par:par + 64, 0:512],
                                    PB[bk][par:par + 64, 0:512]))
            self.pb_free[bk] = t_op
            return t_op

        for cc in range(NCH):
            gc = self.gch
            waits = [pt for toks in pair_tok for pt in toks] if cc == 0 else []
            t_wcm = st("t", waits + [self.bank(0), t_s or (S["V"], 0)],
                       lambda t, cc=cc, gc=gc: [
                t.matmul(PB[0][64 * u:64 * u + 64, p * 64:(p + 1) * 64],
                         b["Y1T"][wp][p][64 * u:64 * u + 64,
                                         cc * 64:(cc + 1) * 64],
                         b["Ssb"][(gc - 1) % 2][64 * u:64 * u + 64,
                                                p * 64:(p + 1) * 64],
                         start=(p == 0), stop=(p == NP - 1),
                         tile_position=(64 * u, 64 * u))
                for u in range(2) for p in range(NP)][-1])
            t_wc[gc % 2] = st("v", [t_wcm], lambda v, cc=cc, gc=gc:
                      v.tensor_sub(
                          b["Wcb"][gc % 2][:, 0:NP * 64]
                          .rearrange("p (a q) -> p a q", a=NP),
                          PB[0][:, 0:NP * 64].rearrange("p (a q) -> p a q",
                                                        a=NP),
                          b["X"][wp][:, 0:].rearrange(
                              "p (a c q) -> p a c q", a=NP, c=NCH)
                          [:, :, cc, 64:128]))
            self.pb_free[0] = t_wc[gc % 2]

            if pend is not None:
                t_o = emit_om(*pend)

            t_sm = st("t", [t_wc[gc % 2], self.bank(1)], lambda t, cc=cc,
                      gc=gc: [
                t.matmul(PB[1][64 * u:64 * u + 64, p * 64:(p + 1) * 64],
                         b["KT"][wp][p][64 * u:64 * u + 64,
                                        cc * 64: cc * 64 + 64],
                         b["Wcb"][gc % 2][64 * u:64 * u + 64,
                                          p * 64:(p + 1) * 64],
                         start=(p == 0), stop=(p == NP - 1),
                         tile_position=(64 * u, 64 * u))
                for u in range(2) for p in range(NP)][-1])
            t_s = st("v", [t_sm], lambda v, cc=cc, gc=gc: (
                [v.scalar_tensor_tensor(
                    b["Sst"][:, p * 64:(p + 1) * 64],
                    b["Sst"][:, p * 64:(p + 1) * 64],
                    b["gCcol"][wp][:, cc * NP + p: cc * NP + p + 1],
                    PB[1][:, p * 64:(p + 1) * 64],
                    ALU.mult, ALU.subtract) for p in range(NP)],
                v.tensor_copy(b["Ssb"][gc % 2][:], b["Sst"][:]))[-1])
            self.pb_free[1] = t_s
            pend = (cc, gc)
            self.gch += 1
            yield

        t_o = emit_om(*pend)
        yield

        for i in range(NTW):
            tD = st("v", [t_o, t_vtok, tcc, self.tok(S["tp"])],
                    lambda v, i=i: (
                v.tensor_mul(b["on2"][:, 0:512].rearrange("p (h q) -> p h q", h=8),
                             b["vtok"][i][:].rearrange("p (h q) -> p h q", h=8),
                             b["coefcol"][:, 8 * i:8 * i + 8]
                             .rearrange("p (h o) -> p h o", o=1)
                             .broadcast_to([128, 8, 64])),
                v.tensor_add(b["otok"][i][:], b["otok"][i][:], b["on2"][:, 0:512]))[-1])
            tsq = st("v", [tD], lambda v, i=i:
                     v.tensor_mul(b["on2"][:, 0:512], b["otok"][i][:],
                                  b["otok"][i][:]))
            trd = st("v", [tsq], lambda v: v.tensor_reduce(
                b["orst"][:], b["on2"][:, 0:512]
                .rearrange("p (h q) -> p h q", h=8),
                AX.X, ALU.add))
            tsr = st("a", [trd], lambda a:
                     a.activation(b["orst"][:], b["orst"][:], AF.Sqrt,
                                  bias=1e-6, scale=1.0 / 64))
            trc = st("v", [tsr], lambda v: v.reciprocal(b["orst"][:], b["orst"][:]))
            tn1 = st("v", [trc], lambda v, i=i: (
                v.tensor_mul(b["on2"][:, 0:512].rearrange("p (h q) -> p h q", h=8),
                             b["otok"][i][:].rearrange("p (h q) -> p h q", h=8),
                             b["orst"][:].rearrange("p (h o) -> p h o", o=1)
                             .broadcast_to([128, 8, 64])),
                v.tensor_mul(b["on2"][:, 0:512], b["on2"][:, 0:512],
                             b["onormT"][:]))[-1])
            tsi = st("a", [tn1], lambda a:
                     a.activation(b["osil"][:], b["on2"][:, 0:512], AF.Silu))
            to2 = st("v", [tsi, t_gt, getattr(self, "war_o2f", (S["tp"], 0))],
                     lambda v, i=i:
                     v.tensor_mul(b["o2f"][:, 0:512], b["osil"][:],
                                  b["gtokT"][:, i * 512:(i + 1) * 512]))
            self.step("s", [to2, getattr(self, "war_o2ff", (S["T"], 0)),
                            self.tok(S["tp"])],
                      lambda sy: (
                self._tp(sy, b["o2ff"][:].rearrange("p (c t) -> p c t", c=4),
                         b["o2f"][:, 0:512]), None)[-1], dma=True)
            ttp = self.tok(S["tp"])
            self.war_o2f = ttp

            for kdh in range(2):
                bk = kdh
                twm = st("t", [ttp, self.bank(bk), self.wldg], lambda t, kdh=kdh, bk=bk: [
                    t.matmul(PB[bk][:, 0:512],
                             b["o2ff"][:, hv * 128:(hv + 1) * 128],
                             b["wo_s"][:, hv * 1024 + kdh * 512:
                                       hv * 1024 + kdh * 512 + 512],
                             start=(hv == 0), stop=(hv == 3))
                    for hv in range(4)][-1])
                po = ((w % self.GW) * NTW + i) * 1024 + kdh * 512
                tev = st("v", [twm, getattr(self, "war_pout", (S["st"], 0))],
                         lambda v, po=po, bk=bk:
                         v.tensor_copy(b["pstage"][:, po:po + 512],
                                       PB[bk][:, 0:512]))
                self.pb_free[bk] = tev
            self.war_o2ff = self.tok(S["T"])
            t_po = tev

            if i == NTW - 1 and w % self.GW == self.GW - 1:
                GJ = self.GW * NTW
                _stf = self.tok(S["st"])
                def store(sy, w=w, t_po=t_po, _stf=_stf, GJ=GJ):
                    self._wait(sy, _stf)
                    self._wait(sy, t_po)
                    pv = self.psh.ap().rearrange(
                        "s (g j p) h -> g s p j h", j=GJ, p=128)
                    ins = sy.dma_start(
                        out=pv[w // self.GW][self.dynv["oslot"]],
                        in_=b["pstage"][:].rearrange("p (j h) -> p j h", j=GJ))
                    self._inc(ins, S["st"], 16)
                self.blk.sync(store)
                self.war_pout = self.tok(S["st"])
            yield

    def _x_phase(self):
        nc, S, PB, cs = self.nc, self.S, self.PB, self.cs
        TO, NTO = self.TO, self.NTO
        st = self.step
        xtok, xsq, statc = self.xtok, self.xsq, self.statc
        GX = min(4, NTO)
        xps = ExitStack()
        pshb2 = [xps.enter_context(
            nc.sbuf_tensor(f"pshb{j}", [128, 2 * GX * 1024], F32))
            for j in range(2)]
        self.rowsc = xps.enter_context(
            nc.sbuf_tensor("rowsc", [16, 1024], F32))
        with nc.Block() as blk:
            self.blk = blk

            def _gsnap(gp):
                pid = gp.partition_id()
                self.dynv["rb"] = gp.snap(pid % 2 * TO, min_val=0, max_val=TO)
            self.blk.gpsimd(_gsnap)

            xw = [xps.enter_context(nc.sbuf_tensor(f"xw{j}", [128, HID], F32))
                  for j in range(2)]
            xq2 = [xps.enter_context(nc.sbuf_tensor(f"xq2_{j}", [128, HID],
                                                    F32)) for j in range(2)]
            t_prev = {0: (S["A"], 0), 1: (S["A"], 0)}
            war_stx = {0: (S["st"], 0), 1: (S["st"], 0)}
            war_xh = {0: (S["T"], 0), 1: (S["T"], 0)}
            t_gld = (S["gld"], 0)
            for tt in range(NTO):
                par = tt % 2
                sc = par * 8
                if tt % GX == 0:
                    g = tt // GX
                    wpb = getattr(self, "war_pshb", {})
                    self.war_pshb = wpb
                    t_gld = self.dma("g", [wpb.get(g % 2, (S["V"], 0))],
                                     lambda gp, g=g: [
                        gp.dma_start(
                            out=pshb2[g % 2]
                            [:, s * GX * 1024:(s + 1) * GX * 1024]
                            .rearrange("p (i h) -> p i h", i=GX),
                            in_=self.psh.ap()[s]
                            [bass.ds(self.dynv["rb"], TO), :]
                            [128 * GX * g:128 * GX * (g + 1), :]
                            .rearrange("(i p) h -> p i h", p=128))
                        for s in range(2)])
                pshb = pshb2[(tt // GX) % 2]
                def ld(sy, tt=tt, par=par):
                    ins = sy.dma_start(
                        out=xtok[par][:],
                        in_=self.xh_in[128 * tt:128 * (tt + 1), :])
                    self._inc(ins, S["ld"], 16)
                self.blk.sync(lambda sy, tt=tt, par=par:
                              (self._wait(sy, t_prev[par]), ld(sy, tt, par)))
                t_ld = self.tok(S["ld"])

                ps0 = (tt % GX) * 1024
                ps1 = GX * 1024 + (tt % GX) * 1024
                tv = st("v", [t_gld, war_stx[par]],
                        lambda v, ps0=ps0, ps1=ps1, par=par, sc=sc: (
                    v.tensor_add(xw[par][:], pshb[:, ps0:ps0 + 1024],
                                 pshb[:, ps1:ps1 + 1024]),
                    v.tensor_mul(xq2[par][:], xw[par][:], xw[par][:]),
                    v.reduce_sum(statc[:, sc:sc + 1], xw[par][:], AX.X),
                    v.reduce_sum(statc[:, sc + 1:sc + 2], xq2[par][:], AX.X),
                    v.tensor_scalar_mul(statc[:, sc:sc + 1],
                                        statc[:, sc:sc + 1], 1.0 / HID),
                    v.tensor_scalar_mul(statc[:, sc + 1:sc + 2],
                                        statc[:, sc + 1:sc + 2], 1.0 / HID),
                    v.scalar_tensor_tensor(statc[:, sc + 2:sc + 3],
                                           statc[:, sc:sc + 1],
                                           statc[:, sc:sc + 1],
                                           statc[:, sc + 1:sc + 2],
                                           ALU.mult, ALU.subtract))[-1])
                if tt % GX == GX - 1:
                    self.war_pshb[(tt // GX) % 2] = tv
                ta = st("a", [tv], lambda a, sc=sc:
                        a.activation(statc[:, sc + 3:sc + 4],
                                     statc[:, sc + 2:sc + 3], AF.Sqrt,
                                     bias=1e-5, scale=-1.0))
                tf = st("v", [ta, t_ld, war_xh[par]],
                        lambda v, par=par, sc=sc: (
                    v.reciprocal(statc[:, sc + 3:sc + 4],
                                 statc[:, sc + 3:sc + 4]),
                    v.tensor_scalar(xw[par][:], xw[par][:],
                                    statc[:, sc:sc + 1],
                                    statc[:, sc + 3:sc + 4],
                                    ALU.subtract, ALU.mult),
                    v.tensor_add(xw[par][:], xw[par][:], xtok[par][:]),
                    v.tensor_copy(self.xhatt[par][:], xw[par][:]))[-1])
                t_prev[par] = tf

                _stf = self.tok(S["st"])
                def stx(sy, tt=tt, tf=tf, _stf=_stf, par=par):
                    self._wait(sy, _stf)
                    self._wait(sy, tf)
                    ins = sy.dma_start(out=self.x2buf.ap()
                                       [128 * tt:128 * (tt + 1), :],
                                       in_=xw[par][:])
                    self._inc(ins, S["st"], 16)
                self.blk.sync(stx)
                war_stx[par] = self.tok(S["st"])

                tcm = st("t", [tf, self.bank(2) if tt == 0 else (S["V"], 0)],
                         lambda t, tt=tt, par=par: (
                    t.matmul(PB[2][0:1, 0:512], cs["cONESC"][:],
                             self.xhatt[par][:, 0:512], start=(tt == 0),
                             stop=(tt == NTO - 1)),
                    t.matmul(PB[3][0:1, 0:512], cs["cONESC"][:],
                             self.xhatt[par][:, 512:1024], start=(tt == 0),
                             stop=(tt == NTO - 1)))[-1])
                war_xh[par] = tcm
            tse = st("v", [tcm], lambda v: (
                v.tensor_copy(self.rowsc[0:1, 0:512], PB[2][0:1, 0:512]),
                v.tensor_copy(self.rowsc[0:1, 512:1024], PB[3][0:1, 0:512]))[-1])
            self.pb_free[2] = tse
            self.pb_free[3] = tse

            _stf = self.tok(S["st"])
            def stsum(sy, tse=tse, _stf=_stf):
                self._wait(sy, _stf)
                self._wait(sy, tse)
                pid = sy.partition_id()
                slot = sy.snap(pid % 2, min_val=0, max_val=1)
                ins = sy.dma_start(out=self.shsum.ap()[slot],
                                   in_=self.rowsc[0:1, 0:1024])
                self._inc(ins, S["st"], 16)
            self.blk.sync(stsum)
            self.blk.gpsimd(lambda gp: gp.wait_ge(S["st"],
                                                  self.cnt.get(S["st"].name, 0)))
        xps.close()

    def _f_phase(self):
        nc, S, PB, cs = self.nc, self.S, self.PB, self.cs
        TO, NTO = self.TO, self.NTO
        NFW = TO // 512 if TO >= 512 else 1
        FW = min(512, TO)
        st = self.step
        ffs = ExitStack()
        sb = lambda n, s, d=BF16: ffs.enter_context(nc.sbuf_tensor(n, s, d))
        fw1 = sb("fw1_s", [128, 8 * 1024])
        fw2 = sb("fw2_s", [128, 8 * 512])
        fw3 = sb("fw3_s", [128, 4 * 256])
        fw4 = sb("fw4_s", [128, 2 * 1])
        gw1 = sb("gw1_s", [128, 8 * 512])
        gw2 = sb("gw2_s", [128, 4 * 256])
        gw3 = sb("gw3_s", [128, 2 * 1])
        frows = sb("frows", [1, 2048 + 64], F32)
        fbcol = sb("fbcol", [128, 14], F32)
        fb4v = sb("fb4v", [1, 8], F32)
        gb1r = sb("gb1r", [1, 512], F32)
        gb2r = sb("gb2r", [1, 256], F32)
        outA = sb("outA", [128, 8 * FW])
        outB = sb("outB", [128, 8 * FW])
        x2f = sb("x2f", [128, 8 * FW])
        h1 = sb("h1", [128, 8 * 512])
        h2 = sb("h2", [128, 4 * 512])
        h3 = sb("h3", [128, 2 * 512])
        tfr = sb("tfr", [1, 512], F32)
        cplr = sb("cplr", [1, 512], F32)
        cplrb = sb("cplrb", [1, 512])
        cplP = sb("cplP", [128, 512])
        cplN = sb("cplN", [128, 512])
        fsc = sb("fsc", [128, 512])
        grow = sb("grow", [1, 1024], F32)
        growb = sb("growb", [1, 1024])
        gcolT = sb("gcolT", [128, 16])
        lnr = sb("lnr", [1, 512], F32)
        lnm = sb("lnm", [1, 512], F32)
        lne = sb("lne", [1, 512], F32)
        lnu = sb("lnu", [1, 512], F32)
        lns = sb("lns", [1, 512], F32)
        lnrb = sb("lnrb", [2, 512])
        lnr2b = sb("lnr2b", [1, 512])
        fsq = sb("fsq", [128, 512])
        xo3 = sb("xo3", [128, 512], F32)

        with nc.Block() as blk:
            self.blk = blk
            t_w = self.dma("g", [], lambda gp: [
                gp.dma_start(out=fw1[:].rearrange("p (a b) -> p a b", a=8),
                             in_=self.ffn_in["fw1"].rearrange("a p b -> p a b")),
                gp.dma_start(out=fw2[:].rearrange("p (a b) -> p a b", a=8),
                             in_=self.ffn_in["fw2"].rearrange("a p b -> p a b")),
                gp.dma_start(out=fw3[:].rearrange("p (a b) -> p a b", a=4),
                             in_=self.ffn_in["fw3"].rearrange("a p b -> p a b")),
                gp.dma_start(out=fw4[:].rearrange("p (a b) -> p a b", a=2),
                             in_=self.ffn_in["fw4"].rearrange("a p b -> p a b"),
                             allow_slow_non_contiguous=True),
                gp.dma_start(out=gw1[:].rearrange("p (a b) -> p a b", a=8),
                             in_=self.ffn_in["gw1"].rearrange("a p b -> p a b")),
                gp.dma_start(out=gw2[:].rearrange("p (a b) -> p a b", a=4),
                             in_=self.ffn_in["gw2"].rearrange("a p b -> p a b")),
                gp.dma_start(out=gw3[:].rearrange("p (a b) -> p a b", a=2),
                             in_=self.ffn_in["gw3"].rearrange("a p b -> p a b"),
                             allow_slow_non_contiguous=True),
                gp.dma_start(out=frows[0:1, 0:1024], in_=self.ffn_in["fb1"]),
                gp.dma_start(out=frows[0:1, 1024:1536], in_=self.ffn_in["fb2"]),
                gp.dma_start(out=frows[0:1, 1536:1792], in_=self.ffn_in["fb3"]),
                gp.dma_start(out=fb4v[0:1, 0:1], in_=self.ffn_in["fb4"]),
                gp.dma_start(out=fb4v[0:1, 1:2], in_=self.ffn_in["gb3"]),
                gp.dma_start(out=fb4v[0:1, 2:3], in_=self.ffn_in["arange"]),
                gp.dma_start(out=gb1r[0:1, :], in_=self.ffn_in["gb1"]),
                gp.dma_start(out=gb2r[0:1, :], in_=self.ffn_in["gb2"]),
            ])

            prow = ffs.enter_context(nc.sbuf_tensor("prow", [1, 1024], F32))
            prow2 = ffs.enter_context(nc.sbuf_tensor("prow2", [1, 1024], F32))
            t_ps = self.dma("s", [], lambda sy: [
                sy.dma_start(out=prow[0:1, :], in_=self.shsum.ap()[0]),
                sy.dma_start(out=prow2[0:1, :], in_=self.shsum.ap()[1])])

            tg1 = st("v", [t_ps, t_w], lambda v: (
                v.tensor_add(grow[0:1, :], prow2[0:1, :],
                             prow[0:1, :]),
                v.tensor_scalar_mul(grow[0:1, :], grow[0:1, :], 1.0 / self.T),
                v.tensor_copy(growb[0:1, :], grow[0:1, :]))[-1])
            tg2 = st("t", [tg1, self.bank(4)], lambda t: [
                t.matmul(PB[4][:, j:j + 1], growb[0:1, 128 * j:128 * (j + 1)],
                         cs["cONES8"][0:1, 0:1], start=(j == 0), stop=(j == 7))
                for j in range(8)][-1])
            tg3 = st("v", [tg2], lambda v:
                     v.tensor_copy(gcolT[:, 0:8], PB[4][:, 0:8]))
            self.pb_free[4] = tg3
            tg4 = st("t", [tg3, self.bank(5)], lambda t: [
                t.matmul(PB[5][0:1, 0:512], gcolT[:, dc:dc + 1],
                         gw1[:, dc * 512:(dc + 1) * 512],
                         start=(dc == 0), stop=(dc == 7)) for dc in range(8)][-1])
            tg5 = st("v", [tg4], lambda v:
                     v.tensor_add(lnr[0:1, 0:512], PB[5][0:1, 0:512], gb1r[0:1, :]))
            self.pb_free[5] = tg5
            tg6 = st("a", [tg5], lambda a:
                     a.activation(lnr[0:1, 0:512], lnr[0:1, 0:512], AF.Gelu))
            tg7 = st("v", [tg6], lambda v:
                     v.tensor_copy(lnrb[0:1, 0:512], lnr[0:1, 0:512]))
            tg8 = st("t", [tg7, self.bank(4)], lambda t: [
                t.matmul(PB[4][:, 8 + j:9 + j], lnrb[0:1, 128 * j:128 * (j + 1)],
                         cs["cONES8"][0:1, 0:1], start=(j == 0), stop=(j == 3))
                for j in range(4)][-1])
            tg9 = st("v", [tg8], lambda v:
                     v.tensor_copy(gcolT[:, 8:12], PB[4][:, 8:12]))
            self.pb_free[4] = tg9
            tg10 = st("t", [tg9, self.bank(5)], lambda t: [
                t.matmul(PB[5][0:1, 0:256], gcolT[:, 8 + dc:9 + dc],
                         gw2[:, dc * 256:(dc + 1) * 256],
                         start=(dc == 0), stop=(dc == 3)) for dc in range(4)][-1])
            tg11 = st("v", [tg10], lambda v:
                      v.tensor_add(lnr[0:1, 0:256], PB[5][0:1, 0:256],
                                   gb2r[0:1, :]))
            self.pb_free[5] = tg11
            tg12 = st("a", [tg11], lambda a:
                      a.activation(lnr[0:1, 0:256], lnr[0:1, 0:256], AF.Gelu))
            tg13 = st("v", [tg12], lambda v:
                      v.tensor_copy(lnrb[0:1, 0:256], lnr[0:1, 0:256]))
            tg14 = st("t", [tg13, self.bank(4)], lambda t: [
                t.matmul(PB[4][:, 12 + j:13 + j], lnrb[0:1, 128 * j:128 * (j + 1)],
                         cs["cONES8"][0:1, 0:1], start=(j == 0), stop=(j == 1))
                for j in range(2)][-1])
            tg15 = st("v", [tg14], lambda v:
                      v.tensor_copy(gcolT[:, 12:14], PB[4][:, 12:14]))
            self.pb_free[4] = tg15
            tg16 = st("t", [tg15, self.bank(5)], lambda t: [
                t.matmul(PB[5][0:1, 0:1], gcolT[:, 12 + dc:13 + dc],
                         gw3[:, dc:dc + 1], start=(dc == 0), stop=(dc == 1))
                for dc in range(2)][-1])
            tg17 = st("v", [tg16], lambda v:
                      v.tensor_scalar_add(fb4v[0:1, 3:4], PB[5][0:1, 0:1],
                                          fb4v[0:1, 1:2]))
            self.pb_free[5] = tg17
            tg18 = st("a", [tg17], lambda a:
                      a.activation(fb4v[0:1, 3:4], fb4v[0:1, 3:4], AF.Sigmoid))
            t_gf = st("v", [tg18], lambda v: (
                v.tensor_scalar_mul(fb4v[0:1, 4:5], fb4v[0:1, 3:4], 0.7),
                v.tensor_scalar_mul(fb4v[0:1, 5:6], fb4v[0:1, 2:3], 2.0))[-1])

            tb1 = st("v", [t_w], lambda v: (
                v.tensor_copy(lnrb[0:1, 0:512], frows[0:1, 0:512]),
                v.tensor_copy(lnr2b[0:1, 0:512], frows[0:1, 512:1024]))[-1])
            tb2 = st("t", [tb1, self.bank(4)], lambda t: (
                [t.matmul(PB[4][:, j:j + 1], lnrb[0:1, 128 * j:128 * (j + 1)],
                          cs["cONES8"][0:1, 0:1], start=(j == 0), stop=False)
                 for j in range(4)],
                [t.matmul(PB[4][:, 4 + j:5 + j], lnr2b[0:1, 128 * j:128 * (j + 1)],
                          cs["cONES8"][0:1, 0:1], start=False, stop=False)
                 for j in range(4)])[-1][-1])
            tb3 = st("v", [tb2], lambda v:
                     v.tensor_copy(lnrb[0:1, 0:512], frows[0:1, 1024:1536]))
            tb4 = st("t", [tb3], lambda t: (
                [t.matmul(PB[4][:, 8 + j:9 + j], lnrb[0:1, 128 * j:128 * (j + 1)],
                          cs["cONES8"][0:1, 0:1], start=False, stop=False)
                 for j in range(4)])[-1])
            tb5 = st("v", [tb4], lambda v:
                     v.tensor_copy(lnrb[0:1, 0:256], frows[0:1, 1536:1792]))
            tb6 = st("t", [tb5], lambda t: (
                [t.matmul(PB[4][:, 12 + j:13 + j],
                          lnrb[0:1, 128 * j:128 * (j + 1)],
                          cs["cONES8"][0:1, 0:1], start=False, stop=(j == 1))
                 for j in range(2)])[-1])
            tb7 = st("v", [tb6], lambda v:
                     v.tensor_copy(fbcol[:, 0:14], PB[4][:, 0:14]))
            self.pb_free[4] = tb7

            statc = self.statc
            fxq = [ffs.enter_context(nc.sbuf_tensor(f"fxq{j}", [128, HID],
                                                    F32)) for j in range(2)]
            xhR = ffs.enter_context(nc.sbuf_tensor("xhR", [128, HID], BF16))
            t_prevF = {0: [(S["A"], 0)], 1: [(S["A"], 0)]}
            war_tpA = {0: (S["tp"], 0), 1: (S["tp"], 0)}
            NTF = FW // 128
            for fw in range(NFW):
                fo = fw * FW
                for tl in range(NTF):
                    tt = fw * NTF + tl
                    par = tt % 2
                    sc = par * 8
                    t_ld = self.dma("s", list(t_prevF[par]),
                                    lambda sy, tt=tt, par=par: [
                        sy.dma_start(out=self.xtok[par][:],
                                     in_=self.x2buf.ap()
                                     [128 * tt:128 * (tt + 1), :])])
                    tv = st("v", [t_ld], lambda v, par=par, sc=sc: (
                        v.tensor_mul(fxq[par][:], self.xtok[par][:],
                                     self.xtok[par][:]),
                        v.reduce_sum(statc[:, sc:sc + 1], self.xtok[par][:],
                                     AX.X),
                        v.reduce_sum(statc[:, sc + 1:sc + 2], fxq[par][:],
                                     AX.X),
                        v.tensor_scalar_mul(statc[:, sc:sc + 1],
                                            statc[:, sc:sc + 1], 1.0 / HID),
                        v.tensor_scalar_mul(statc[:, sc + 1:sc + 2],
                                            statc[:, sc + 1:sc + 2],
                                            1.0 / HID),
                        v.scalar_tensor_tensor(statc[:, sc + 2:sc + 3],
                                               statc[:, sc:sc + 1],
                                               statc[:, sc:sc + 1],
                                               statc[:, sc + 1:sc + 2],
                                               ALU.mult, ALU.subtract))[-1])
                    ta = st("a", [tv], lambda a, sc=sc:
                            a.activation(statc[:, sc + 3:sc + 4],
                                         statc[:, sc + 2:sc + 3], AF.Sqrt,
                                         bias=1e-5, scale=-1.0))
                    tf = st("v", [ta, war_tpA[par]],
                            lambda v, par=par, sc=sc: (
                            v.reciprocal(statc[:, sc + 3:sc + 4],
                                         statc[:, sc + 3:sc + 4]),
                            v.tensor_scalar(self.xhatt[par][:],
                                            self.xtok[par][:],
                                            statc[:, sc:sc + 1],
                                            statc[:, sc + 3:sc + 4],
                                            ALU.subtract, ALU.mult))[-1])
                    tcp = st("v", [t_ld, self.tok(S["tp"])],
                             lambda v, par=par:
                             v.tensor_copy(xhR[:], self.xtok[par][:]))
                    t_prevF[par] = [tf, tcp]

                    _tpf2 = self.tok(S["tp"])
                    war_f = getattr(self, "war_fentry", [])
                    def tps(sy, tl=tl, tf=tf, tcp=tcp, _tpf2=_tpf2,
                            war_f=war_f, par=par):
                        self._wait(sy, _tpf2)
                        self._wait(sy, tf)
                        self._wait(sy, tcp)
                        for tk in war_f:
                            self._wait(sy, tk)
                        self._tp(sy, outA[:].rearrange("p (c q) -> p c q", c=8)
                                 [:, :, 128 * tl:128 * (tl + 1)],
                                 self.xhatt[par][:])
                        self._tp(sy, x2f[:].rearrange("p (c q) -> p c q", c=8)
                                 [:, :, 128 * tl:128 * (tl + 1)],
                                 xhR[:])
                    self.blk.sync(tps)
                    war_tpA[par] = self.tok(S["tp"])
                t_entry = self.tok(S["tp"])

                IN, OUT = outA, outB
                t_tw = t_entry
                for cyc in range(3):
                    for oc in range(8):
                        bk = oc % 4
                        tm = st("t", [self.bank(bk)] +
                                ([t_tw, t_w, tb7, t_gf] if oc == 0 else []),
                                lambda t, oc=oc, bk=bk, fo=fo: [
                            t.matmul(PB[bk][:, 0:FW],
                                     fw1[:, dc * 1024 + oc * 128:
                                         dc * 1024 + oc * 128 + 128],
                                     IN[:, dc * FW:(dc + 1) * FW],
                                     start=(dc == 0), stop=(dc == 7))
                            for dc in range(8)][-1])
                        te = st("a", [tm], lambda a, oc=oc, bk=bk:
                                a.activation(h1[:, oc * 512:oc * 512 + FW],
                                             PB[bk][:, 0:FW], AF.Gelu,
                                             bias=fbcol[:, oc:oc + 1]))
                        self.pb_free[bk] = te
                    t_h1 = self.tok(S["A"])
                    for oc in range(4):
                        bk = oc % 4
                        tm = st("t", [self.bank(bk)] + ([t_h1] if oc == 0 else []),
                                lambda t, oc=oc, bk=bk: [
                            t.matmul(PB[bk][:, 0:FW],
                                     fw2[:, dc * 512 + oc * 128:
                                         dc * 512 + oc * 128 + 128],
                                     h1[:, dc * 512:dc * 512 + FW],
                                     start=(dc == 0), stop=(dc == 7))
                            for dc in range(8)][-1])
                        te = st("a", [tm], lambda a, oc=oc, bk=bk:
                                a.activation(h2[:, oc * 512:oc * 512 + FW],
                                             PB[bk][:, 0:FW], AF.Gelu,
                                             bias=fbcol[:, 4 + oc:5 + oc]))
                        self.pb_free[bk] = te
                    t_h2 = self.tok(S["A"])
                    for oc in range(2):
                        bk = oc % 2
                        tm = st("t", [self.bank(bk)] + ([t_h2] if oc == 0 else []),
                                lambda t, oc=oc, bk=bk: [
                            t.matmul(PB[bk][:, 0:FW],
                                     fw3[:, dc * 256 + oc * 128:
                                         dc * 256 + oc * 128 + 128],
                                     h2[:, dc * 512:dc * 512 + FW],
                                     start=(dc == 0), stop=(dc == 3))
                            for dc in range(4)][-1])
                        te = st("a", [tm], lambda a, oc=oc, bk=bk:
                                a.activation(h3[:, oc * 512:oc * 512 + FW],
                                             PB[bk][:, 0:FW], AF.Gelu,
                                             bias=fbcol[:, 12 + oc:13 + oc]))
                        self.pb_free[bk] = te
                    t_h3 = self.tok(S["A"])
                    tm4 = st("t", [te, t_h3, self.bank(6)], lambda t: [
                        t.matmul(PB[6][0:1, 0:FW], fw4[:, dc:dc + 1],
                                 h3[:, dc * 512:dc * 512 + FW],
                                 start=(dc == 0), stop=(dc == 1))
                        for dc in range(2)][-1])
                    tt4 = st("v", [tm4, t_gf], lambda v:
                             v.tensor_scalar_add(tfr[0:1, 0:FW], PB[6][0:1, 0:FW],
                                                 fb4v[0:1, 0:1]))
                    self.pb_free[6] = tt4
                    tt5 = st("a", [tt4], lambda a:
                             a.activation(tfr[0:1, 0:FW], tfr[0:1, 0:FW],
                                          AF.Sigmoid))
                    tt6 = st("v", [tt5], lambda v: (
                        v.tensor_scalar(cplr[0:1, 0:FW], tfr[0:1, 0:FW], 0.3,
                                        fb4v[0:1, 4:5], ALU.mult, ALU.add),
                        v.tensor_scalar(cplr[0:1, 0:FW], cplr[0:1, 0:FW], -0.5,
                                        fb4v[0:1, 5:6], ALU.add, ALU.mult),
                        v.tensor_scalar_add(cplr[0:1, 0:FW], cplr[0:1, 0:FW], 0.1),
                        v.tensor_copy(cplrb[0:1, 0:FW], cplr[0:1, 0:FW]))[-1])
                    tt7 = st("t", [tt6, self.bank(7)], lambda t:
                             t.matmul(PB[7][:, 0:FW], cs["cONESR"][:],
                                      cplrb[0:1, 0:FW], start=True, stop=True))
                    tt8 = st("v", [tt7], lambda v: (
                        v.tensor_copy(cplP[:, 0:FW], PB[7][:, 0:FW]),
                        v.tensor_scalar_mul(cplN[:, 0:FW], PB[7][:, 0:FW], -1.0))[-1])
                    self.pb_free[7] = tt8
                    sgn = {0: 1, 1: 1, 2: -1, 3: -1, 4: -1, 5: -1, 6: 1, 7: 1}
                    t_tw = st("v", [tt8], lambda v: [
                        (v.tensor_mul(fsc[:, 0:FW],
                                      IN[:, ((c + 4) % 8) * FW:
                                         ((c + 4) % 8 + 1) * FW],
                                      (cplP if sgn[c] > 0 else cplN)[:, 0:FW]),
                         v.tensor_add(OUT[:, c * FW:(c + 1) * FW],
                                      IN[:, c * FW:(c + 1) * FW],
                                      fsc[:, 0:FW]))[-1]
                        for c in range(8)][-1])
                    IN, OUT = OUT, IN
                FIN = IN

                tm = st("t", [t_tw, self.bank(2), self.bank(3)], lambda t: [
                    t.matmul(PB[2][0:1, 0:FW], cs["cONESC"][:],
                             FIN[:, c * FW:(c + 1) * FW],
                             start=(c == 0), stop=(c == 7))
                    for c in range(8)][-1])
                tsq_l = []
                for c in range(8):
                    ta = st("a", [tm if c == 0 else tsq_l[-1]],
                            lambda a, c=c:
                            a.activation(fsq[:, 0:FW],
                                         FIN[:, c * FW:(c + 1) * FW],
                                         AF.Square))
                    tb = st("t", [ta], lambda t, c=c:
                            t.matmul(PB[3][0:1, 0:FW], cs["cONESC"][:],
                                     fsq[:, 0:FW], start=(c == 0), stop=(c == 7)))
                    tsq_l.append(tb)
                trow = st("v", [tsq_l[-1]], lambda v: (
                    v.tensor_scalar_mul(lnm[0:1, 0:FW], PB[2][0:1, 0:FW],
                                        1.0 / HID),
                    v.tensor_scalar_mul(lne[0:1, 0:FW], PB[3][0:1, 0:FW],
                                        1.0 / HID),
                    v.tensor_tensor(lnu[0:1, 0:FW], lnm[0:1, 0:FW],
                                    lnm[0:1, 0:FW], ALU.mult),
                    v.tensor_sub(lnu[0:1, 0:FW], lnu[0:1, 0:FW],
                                 lne[0:1, 0:FW]))[-1])
                self.pb_free[2] = trow
                self.pb_free[3] = trow
                tsr2 = st("a", [trow], lambda a:
                          a.activation(lns[0:1, 0:FW], lnu[0:1, 0:FW], AF.Sqrt,
                                       bias=1e-5, scale=-1.0))
                tf2 = st("v", [tsr2], lambda v: (
                    v.reciprocal(lne[0:1, 0:FW], lns[0:1, 0:FW]),
                    v.tensor_tensor(lnu[0:1, 0:FW], lnm[0:1, 0:FW],
                                    lne[0:1, 0:FW], ALU.mult),
                    v.tensor_scalar_mul(lnu[0:1, 0:FW], lnu[0:1, 0:FW], -1.0),
                    v.tensor_copy(lnrb[0:1, 0:FW], lne[0:1, 0:FW]),
                    v.tensor_copy(lnr2b[0:1, 0:FW], lnu[0:1, 0:FW]))[-1])
                tbc1 = st("t", [tf2, self.bank(4), self.bank(5)], lambda t: (
                    t.matmul(PB[4][:, 0:FW], cs["cONESR"][:], lnrb[0:1, 0:FW],
                             start=True, stop=True),
                    t.matmul(PB[5][:, 0:FW], cs["cONESR"][:], lnr2b[0:1, 0:FW],
                             start=True, stop=True))[-1])
                last_store = getattr(self, "war_xo3", (S["st"], 0))
                for c in range(8):
                    tx3 = st("v", [tbc1, last_store], lambda v, c=c: (
                        v.tensor_mul(fsc[:, 0:FW],
                                     FIN[:, c * FW:(c + 1) * FW],
                                     PB[4][:, 0:FW]),
                        v.tensor_add(fsc[:, 0:FW], fsc[:, 0:FW], PB[5][:, 0:FW]),
                        v.tensor_add(xo3[:, 0:FW], fsc[:, 0:FW],
                                     x2f[:, c * FW:(c + 1) * FW]))[-1])
                    _stf = self.tok(S["st"])
                    def sout(sy, c=c, fo=fo, tx3=tx3, _stf=_stf):
                        self._wait(sy, _stf)
                        self._wait(sy, tx3)
                        ins = sy.dma_start(
                            out=self.out_t[128 * c:128 * (c + 1), fo:fo + FW],
                            in_=xo3[:, 0:FW])
                        self._inc(ins, S["st"], 16)
                    self.blk.sync(sout)
                    last_store = self.tok(S["st"])
                    self.war_xo3 = last_store
                self.pb_free[4] = tx3
                self.pb_free[5] = tx3
                self.war_fentry = [t_tw, tx3]
            self.blk.gpsimd(lambda gp: gp.wait_ge(S["st"],
                                                  self.cnt.get(S["st"].name, 0)))
        ffs.close()


def build_program(T=4096, W=256, debug=()):
    return Builder(T, W, debug).build()


def kernel(**inputs):
    x = np.asarray(inputs["L0_attn_x"])
    Bn, T, _ = x.shape
    nc = build_program(T=T, W=min(256, T))
    CSA, CSB = rope_tables(T)
    packs = [pack_attn(inputs, hh) for hh in (0, 1)]
    ffn = pack_ffn(inputs)
    in_maps = []
    for c in range(8):
        b, hh = c // 2, c % 2
        m = dict(x_in=np.ascontiguousarray(x[b]).astype(np.float32),
                 xh_in=np.ascontiguousarray(
                     x[b][hh * (T // 2):(hh + 1) * (T // 2)]).astype(np.float32),
                 csa=CSA, csb=CSB)
        m.update(packs[hh])
        m.update(ffn)
        in_maps.append(m)
    res = run_bass_kernel_spmd(nc, in_maps, core_ids=list(range(8)))
    out = np.empty((Bn, T, HID), np.float32)
    for c in range(8):
        b, hh = c // 2, c % 2
        out[b, hh * (T // 2):(hh + 1) * (T // 2), :] = res.results[c]["out"].T
    return out
